# version 01
# baseline (speedup 1.0000x reference)
\
import math
import os
import numpy as np
import ml_dtypes
from contextlib import ExitStack
import concourse.bass as bass
import concourse.mybir as mybir
from concourse.bass_utils import run_bass_kernel_spmd
from concourse.ap import AP as RawAP

F32 = mybir.dt.float32
BF16 = mybir.dt.bfloat16
AF = mybir.ActivationFunctionType
ALU = mybir.AluOpType
AX = mybir.AxisListType

B, S, D, H = 4, 4096, 4096, 32
TQ = 2048
ISQ = 1.0 / math.sqrt(128.0)

G0 = [7, 6, 5, 4, 3, 2, 1, 0, 24, 25, 26, 27, 28, 29, 30, 31]
G1 = [8, 9, 10, 11, 12, 13, 14, 15, 23, 22, 21, 20, 19, 18, 17, 16]
BOUND = [max(a, b) + 1 for a, b in zip(G0, G1)]
CLEAR = [min(a, b) for a, b in zip(G0, G1)]
SPANOFF = np.cumsum([0] + [(b - c) * 128 for b, c in zip(BOUND, CLEAR)]).tolist()
SPANTOT = int(SPANOFF[-1])

DEBUG = [s for s in os.environ.get("MK_DEBUG", "").split(",") if s]
PHASE = int(os.environ.get("MK_PHASE", "99"))
SER = int(os.environ.get("MK_SER", "0"))
_NCCACHE = {}


class _Cut(Exception):
    pass


def build():
    nc = bass.Bass("TRN2", target_bir_lowering=False, detect_race_conditions=False)
    ctx = ExitStack()

    def dram(name, shape, dtype, kind="Internal"):
        if kind == "Internal" and name in DEBUG:
            kind = "ExternalOutput"
        return nc.dram_tensor(name, list(shape), dtype, kind=kind).ap()

    xkv = dram("xkv", (S, D), F32, "ExternalInput")
    xq = dram("xq", (TQ, D), F32, "ExternalInput")
    qidx = dram("qidx", (128, 16), F32, "ExternalInput")
    wqT = dram("wqT", (D, D), F32, "ExternalInput")
    wkT = dram("wkT", (D, D), F32, "ExternalInput")
    wvT = dram("wvT", (D, D), F32, "ExternalInput")
    woT = dram("woT", (D, D), F32, "ExternalInput")
    biases = dram("biases", (128, 160), F32, "ExternalInput")
    worgT = dram("worgT", (D, D), F32, "ExternalInput")
    downT = dram("downT", (D, 512), F32, "ExternalInput")
    upT = dram("upT", (512, D), F32, "ExternalInput")
    keysTc = dram("keysTc", (128, 8), F32, "ExternalInput")
    wqrT = dram("wqrT", (D, 128), F32, "ExternalInput")
    wbaseT = dram("wbaseT", (D, D), F32, "ExternalInput")
    wgateT = dram("wgateT", (D, 8), F32, "ExternalInput")
    acatT = dram("acatT", (D, 128), F32, "ExternalInput")
    bcatT = dram("bcatT", (128, D), F32, "ExternalInput")
    identc = dram("identc", (128, 128), BF16, "ExternalInput")
    onehotc = dram("onehotc", (128, 64), BF16, "ExternalInput")
    fftc = dram("fftc", (128, 16384), BF16, "ExternalInput")
    fftsm = dram("fftsm", (128, 1408), BF16, "ExternalInput")
    outT = dram("out", (TQ, D), F32, "ExternalOutput")

    wbf = {}
    for nm, shp in [("wq", (D, D)), ("wk", (D, D)), ("wv", (D, D)), ("wo", (D, D)),
                    ("worg", (D, D)), ("wbase", (D, D)), ("down", (D, 512)),
                    ("up", (512, D)), ("wqr", (D, 128)), ("acat", (D, 128)),
                    ("bcat", (128, D))]:
        wbf[nm] = dram(nm + "_bf", shp, BF16)
    p2dbg = dram("p2dbg", (4, 128, 4096), BF16)
    p4dbg = dram("p4dbg", (4, 128, 4096), BF16)
    p4dbgf = dram("p4dbgf", (128, 768), F32)
    p2v = dram("p2v", (2, 128, 4096), BF16)
    p2dbgf = dram("p2dbgf", (2, 128, 768), F32)
    k_sp = dram("k_sp", (H, 128, S), BF16)
    v_sp = dram("v_sp", (H, 128, S), BF16)
    q_sp = dram("q_sp", (H, 128, TQ), BF16)
    xnq_sp = dram("xnq_sp", (32, 128, TQ), BF16)
    ao_sp = dram("ao_sp", (H, 128, TQ), BF16)
    s0_sp = dram("s0_sp", (32, 128, TQ), BF16)
    sg_sp = dram("sg_sp", (TQ, D), BF16)
    x1_sp = dram("x1_sp", (TQ, D), BF16)
    xn1_sp = dram("xn1_sp", (TQ, D), BF16)
    s1_sp = dram("s1_sp", (TQ, D), BF16)
    mean_sp = dram("mean_sp", (8, 512), BF16)

    A = ctx.enter_context(nc.sbuf_tensor("Aa", [128, 65536], BF16))
    Wu = ctx.enter_context(nc.sbuf_tensor("Wu", [128, 16384], BF16))
    Cf = ctx.enter_context(nc.sbuf_tensor("Cf", [128, 8192], F32))
    STT = ctx.enter_context(nc.sbuf_tensor("STT", [128, 768], F32))
    SM = ctx.enter_context(nc.sbuf_tensor("SMM", [128, 2048], BF16))
    IDN = ctx.enter_context(nc.sbuf_tensor("IDN", [128, 128], BF16))
    OH = ctx.enter_context(nc.sbuf_tensor("OHH", [128, 64], BF16))
    ZOH = ctx.enter_context(nc.sbuf_tensor("ZOH", [128, 8], BF16))
    BIA = ctx.enter_context(nc.sbuf_tensor("BIA", [128, 160], F32))
    QIX = ctx.enter_context(nc.sbuf_tensor("QIX", [128, 16], F32))
    CE = ctx.enter_context(nc.sbuf_tensor("CEE", [128, 8], F32))
    OHX = ctx.enter_context(nc.sbuf_tensor("OHX", [128, 1408], BF16))
    QB = ctx.enter_context(nc.sbuf_tensor("QB", [128, 256], BF16))
    PTS = ctx.enter_context(nc.sbuf_tensor("PTS", [128, 1024], BF16))
    pf = [ctx.enter_context(nc.psum_tensor("pf%d" % i, [128, 512], F32))
          for i in range(6)]
    pb = [ctx.enter_context(nc.psum_tensor("pbb%d" % i, [128, 1024], BF16))
          for i in range(2)]

    sems = {}
    for s in ("sx", "sw", "sst", "sT", "sV", "sA", "sG"):
        sems[s] = ctx.enter_context(nc.semaphore(s))
    cnt = {s: 0 for s in sems}
    ops = {k: [] for k in ("sync", "gpsimd", "scalar", "vector", "tensor")}

    def em(eng, fn):
        ops[eng].append(fn)

    def WT(eng, s, val):
        if val > 0:
            em(eng, lambda e, h=sems[s], v=val: e.wait_ge(h, v))

    def LD(out_ap, in_ap, tr=False, slow=False):
        if slow:
            em("sync", lambda e, o=out_ap, i=in_ap:
               e.dma_start(out=o, in_=i,
                           allow_slow_non_contiguous=True).then_inc(sems["sx"], 16))
        else:
            em("sync", lambda e, o=out_ap, i=in_ap, t=tr:
               e.dma_start(out=o, in_=i, transpose=t).then_inc(sems["sx"], 16))
        cnt["sx"] += 16
        return cnt["sx"]

    def LDW(out_ap, in_ap):
        em("gpsimd", lambda e, o=out_ap, i=in_ap:
           e.dma_start(out=o, in_=i).then_inc(sems["sw"], 16))
        cnt["sw"] += 16
        return cnt["sw"]

    def STO(out_ap, in_ap):
        em("sync", lambda e, o=out_ap, i=in_ap:
           e.dma_start(out=o, in_=i).then_inc(sems["sst"], 16))
        cnt["sst"] += 16
        return cnt["sst"]

    def MM(outp, lhsT, rhs, start, stop, inc=False):
        if inc:
            cnt["sT"] += 1
            em("tensor", lambda e, o=outp, l=lhsT, r=rhs, a=start, b=stop:
               e.matmul(o, l, r, start=a, stop=b).then_inc(sems["sT"], 1))
            return cnt["sT"]
        em("tensor", lambda e, o=outp, l=lhsT, r=rhs, a=start, b=stop:
           e.matmul(o, l, r, start=a, stop=b))
        return cnt["sT"]

    def TR(outp, inp):
        cnt["sT"] += 1
        em("tensor", lambda e, o=outp, i=inp:
           e.transpose(o, i, IDN[:]).then_inc(sems["sT"], 1))
        return cnt["sT"]

    def V(fn, inc=True):
        if inc:
            cnt["sV"] += 1
            em("vector", lambda e, f=fn: f(e).then_inc(sems["sV"], 1))
            return cnt["sV"]
        em("vector", lambda e, f=fn: f(e))
        return cnt["sV"]

    def SC(fn, inc=True):
        if inc:
            cnt["sA"] += 1
            em("scalar", lambda e, f=fn: f(e).then_inc(sems["sA"], 1))
            return cnt["sA"]
        em("scalar", lambda e, f=fn: f(e))
        return cnt["sA"]

    def GP(fn):
        cnt["sG"] += 1
        em("gpsimd", lambda e, f=fn: f(e).then_inc(sems["sG"], 1))
        return cnt["sG"]

    def VD():
        em("vector", lambda e: e.drain())

    def SD():
        em("scalar", lambda e: e.drain())

    FCA = Wu[:, 0:16384]
    MFC = [FCA[:, 128 * b:128 * (b + 1)] for b in range(32)]
    MFS = [FCA[:, 4096 + 128 * b:4096 + 128 * (b + 1)] for b in range(32)]
    M2C = [FCA[:, 8192 + 128 * u:8192 + 128 * (u + 1)] for u in range(32)]
    M2S = [FCA[:, 12288 + 128 * u:12288 + 128 * (u + 1)] for u in range(32)]
    BDC = OHX[:, 0:128]
    BDS = OHX[:, 128:256]
    BDNS = OHX[:, 256:384]
    BDIC = OHX[:, 384:512]
    BDIS = OHX[:, 512:640]
    BDINS = OHX[:, 640:768]
    EXP8 = OHX[0:8, 768:896]
    BCs = [OHX[0:8, 896 + 128 * c:896 + 128 * (c + 1)] for c in range(4)]

    LD(IDN[:], identc)
    LD(OH[:], onehotc)
    LD(QIX[:], qidx)
    LD(BIA[:], biases)
    LD(OHX[:], fftsm)
    V(lambda e: e.memset(ZOH[:], 0.0))
    p0_lds = cnt["sx"]

    jobs = []
    for nm, src, ntile, wid in [("wq", wqT, 32, D), ("wk", wkT, 32, D),
                                ("wv", wvT, 32, D), ("wo", woT, 32, D),
                                ("worg", worgT, 32, D), ("wbase", wbaseT, 32, D),
                                ("down", downT, 32, 512), ("up", upT, 4, D),
                                ("wqr", wqrT, 32, 128), ("acat", acatT, 32, 128),
                                ("bcat", bcatT, 1, D)]:
        for t in range(ntile):
            jobs.append((nm, src, t, wid))
    st_after = {}
    for ji, (nm, src, t, wid) in enumerate(jobs):
        stg = A[:, (ji % 2) * 4096:(ji % 2) * 4096 + wid]
        if ji >= 2:
            WT("gpsimd", "sst", st_after[ji - 2])
        wv = LDW(stg, src[t * 128:(t + 1) * 128, :])
        WT("sync", "sw", wv)
        STO(wbf[nm][t * 128:(t + 1) * 128, :], stg)
        st_after[ji] = cnt["sst"]
    p0_done = cnt["sst"]
    WT("sync", "sx", p0_lds)

    def _cut(n):
        if PHASE <= n:
            raise _Cut()

    try:
        _cut(0)
        V(lambda e: e.memset(STT[:, 0:256], 0.0))
        V(lambda e: e.memset(STT[:, 259:260], 1e-5))
        VD()
        zs_v = V(lambda e: e.engine_nop())
        WT("scalar", "sV", zs_v)

        xnfm = A[:, 0:32768]
        xn_tm = A[:, 32768:36864]
        sqd = A[:, 36864:40960]
        wstr = [Wu[:, 0:4096], Wu[:, 4096:8192]]
        kst = [SM[:, 0:512], SM[:, 512:1024]]
        xbufs = [Cf[:, 0:4096]]

        def ln_tile(xbuf, scol, xn_out):
            SC(lambda e, x=xbuf, c=scol: e.activation(
                out=sqd, in_=x, func=AF.Square, accum_out=STT[:, c:c + 1]))
            sum_v = V(lambda e, x=xbuf, c=scol: e.reduce_sum(
                STT[:, 64 + c:65 + c], x, axis=AX.X))
            WT("scalar", "sV", sum_v)
            SC(lambda e, c=scol: e.mul(out=STT[:, 128 + c:129 + c],
                                       in_=STT[:, 64 + c:65 + c], mul=1.0 / 4096.0),
               inc=False)
            SC(lambda e, c=scol: e.mul(out=STT[:, 256:257], in_=STT[:, c:c + 1],
                                       mul=1.0 / 4096.0), inc=False)
            SD()
            SC(lambda e, c=scol: e.activation(out=STT[:, 257:258],
                                              in_=STT[:, 128 + c:129 + c],
                                              func=AF.Square), inc=False)
            SD()
            SC(lambda e: e.mul(out=STT[:, 257:258], in_=STT[:, 257:258], mul=-1.0),
               inc=False)
            SD()
            SC(lambda e: e.add(out=STT[:, 258:259], in_=STT[:, 256:257],
                               add=STT[:, 257:258]), inc=False)
            SD()
            SC(lambda e: e.add(out=STT[:, 258:259], in_=STT[:, 258:259],
                               add=STT[:, 259:260]), inc=False)
            SD()
            SC(lambda e: e.activation(out=STT[:, 258:259], in_=STT[:, 258:259],
                                      func=AF.Ln), inc=False)
            SD()
            rstd_a = SC(lambda e, c=scol: e.activation(
                out=STT[:, 192 + c:193 + c], in_=STT[:, 258:259], func=AF.Exp,
                scale=-0.5))
            WT("vector", "sA", rstd_a)
            return V(lambda e, x=xbuf, c=scol, xo=xn_out: e.tensor_scalar(
                out=xo, in0=x, scalar1=STT[:, 128 + c:129 + c],
                scalar2=STT[:, 192 + c:193 + c], op0=ALU.subtract, op1=ALU.mult))

        tiles = [("kv", t) for t in range(32)] + [("q", t) for t in range(16)]
        xload_v = {}
        kind0, t0_ = tiles[0]
        xload_v[0] = LD(xbufs[0][:], xkv[t0_ * 128:(t0_ + 1) * 128, :])

        mean_open = [False]
        tr_evac_v = {0: 0, 1: 0}
        proj_mm_v = {0: 0, 1: 0}
        kst_sto_v = [0, 0]
        last_tr = [0]

        for ti, (kind, t) in enumerate(tiles):
            xbuf = xbufs[0]
            WT("scalar", "sx", xload_v[ti])
            WT("vector", "sx", xload_v[ti])
            WT("vector", "sT", last_tr[0])
            xn_v = ln_tile(xbuf, ti, xn_tm)
            if ti + 1 < len(tiles):
                kind2, t2 = tiles[ti + 1]
                src2 = (xkv[t2 * 128:(t2 + 1) * 128, :] if kind2 == "kv"
                        else xq[t2 * 128:(t2 + 1) * 128, :])
                WT("sync", "sV", xn_v)
                xload_v[ti + 1] = LD(xbuf[:], src2)
            WT("tensor", "sV", xn_v)
            if kind == "kv":
                for c8 in range(8):
                    MM(pf[5][0:8, :], OH[:, c8 * 8:c8 * 8 + 8],
                       xn_tm[:, c8 * 512:(c8 + 1) * 512],
                       start=(not mean_open[0]), stop=False)
                    mean_open[0] = True
            gpos = ti % 8
            for c in range(32):
                WT("tensor", "sV", tr_evac_v[c % 2])
                trv = TR(pb[c % 2][:, 0:128], xn_tm[:, c * 128:(c + 1) * 128])
                WT("vector", "sT", trv)
                tr_evac_v[c % 2] = V(lambda e, c_=c, g_=gpos: e.tensor_copy(
                    xnfm[:, c_ * 1024 + g_ * 128:c_ * 1024 + g_ * 128 + 128],
                    pb[c_ % 2][:, 0:128]))
            last_tr[0] = cnt["sT"]

            if ti == 31:
                mv = MM(pf[5][0:8, :], ZOH[:], xn_tm[:, 0:512], start=False, stop=True,
                        inc=True)
                WT("vector", "sT", mv)
                mev = V(lambda e: e.tensor_scalar(out=SM[0:8, 1024:1536],
                                                  in0=pf[5][0:8, :],
                                                  scalar1=1.0 / 4096.0, scalar2=None,
                                                  op0=ALU.mult))
                WT("sync", "sV", mev)
                STO(mean_sp[0:8, :], SM[0:8, 1024:1536])

            if ti % 8 == 7:
                grp = ti // 8
                xn_ready = cnt["sV"]
                if kind == "kv":
                    projs = [("wk", k_sp, 2), ("wv", v_sp, 3)]
                    tok0 = grp * 1024
                else:
                    projs = [("wq", q_sp, 1)]
                    tok0 = (grp - 4) * 1024
                WT("sync", "sst", p0_done)
                for wnm, dst_sp, bcol in projs:
                    for oc in range(32):
                        wap = RawAP(wbf[wnm].tensor, oc * 128,
                                    [[4096, 128], [128 * 4096, 32], [1, 128]])
                        WT("sync", "sT", proj_mm_v[oc % 2])
                        lv = LD(wstr[oc % 2][:], wap)
                        WT("tensor", "sx", lv)
                        WT("tensor", "sV", xn_ready)
                        for th in range(2):
                            WT("tensor", "sV", cnt["sV"])
                            for c in range(32):
                                mmv = MM(pf[oc % 2][:],
                                         wstr[oc % 2][:, c * 128:(c + 1) * 128],
                                         xnfm[:, c * 1024 + th * 512:c * 1024 + (th + 1) * 512],
                                         start=(c == 0), stop=(c == 31), inc=(c == 31))
                            proj_mm_v[oc % 2] = mmv
                            WT("vector", "sT", mmv)
                            WT("vector", "sst", kst_sto_v[th])
                            if bcol == 1:
                                ev = V(lambda e, o=oc, h2=th: e.tensor_scalar(
                                    out=kst[h2], in0=pf[o % 2][:],
                                    scalar1=BIA[:, o:o + 1], scalar2=ISQ,
                                    op0=ALU.add, op1=ALU.mult))
                            else:
                                bof = 32 * (bcol - 1)
                                ev = V(lambda e, o=oc, h2=th, bo=bof: e.tensor_scalar(
                                    out=kst[h2], in0=pf[o % 2][:],
                                    scalar1=BIA[:, bo + o:bo + o + 1], scalar2=None,
                                    op0=ALU.add))
                            WT("sync", "sV", ev)
                            kst_sto_v[th] = STO(
                                dst_sp[oc, :, tok0 + th * 512:tok0 + (th + 1) * 512],
                                kst[th])
                    if wnm == "wq":
                        for c in range(32):
                            WT("sync", "sV", xn_ready)
                            STO(xnq_sp[c, :, tok0:tok0 + 1024],
                                xnfm[:, c * 1024:(c + 1) * 1024])
                WT("tensor", "sst", cnt["sst"])

        _cut(1)
        mcol = SM[:, 1536:1568]
        WT("sync", "sst", cnt["sst"])
        LD(mcol, RawAP(mean_sp.tensor, 0, [[1, 128], [128, 32]]), slow=True)
        qrw = Wu[:, 0:4096]
        LD(qrw, RawAP(wbf["wqr"].tensor, 0, [[128, 128], [128 * 128, 32], [1, 128]]))
        kye = QB[:, 0:8]
        WT("gpsimd", "sst", cnt["sst"])
        kv_ = LDW(kye, keysTc)
        rt_ld = cnt["sx"]
        WT("tensor", "sx", rt_ld)
        WT("tensor", "sw", kv_)
        WT("tensor", "sT", proj_mm_v[0])
        qcol_mm = 0
        for c in range(32):
            qcol_mm = MM(pf[0][:, 0:1], qrw[:, c * 128:(c + 1) * 128], mcol[:, c:c + 1],
                         start=(c == 0), stop=(c == 31), inc=(c == 31))
        WT("vector", "sT", qcol_mm)
        qcol_v = V(lambda e: e.tensor_copy(SM[:, 1568:1569], pf[0][:, 0:1]))
        WT("tensor", "sV", qcol_v)
        WT("tensor", "sT", proj_mm_v[1])
        sc_mm = MM(pf[1][0:1, 0:8], SM[:, 1568:1569], kye, start=True, stop=True,
                   inc=True)
        WT("vector", "sT", sc_mm)
        V(lambda e: e.tensor_scalar(out=STT[0:1, 300:308], in0=pf[1][0:1, 0:8],
                                    scalar1=ISQ, scalar2=None, op0=ALU.mult))
        VD()
        V(lambda e: e.reduce_max(STT[0:1, 308:309], STT[0:1, 300:308], axis=AX.X,
                                 negate=True))
        VD()
        rmx_v = V(lambda e: e.engine_nop())
        WT("scalar", "sV", rmx_v)
        SC(lambda e: e.memzero(STT[0:1, 310:311]), inc=False)
        SD()
        ex_a = SC(lambda e: e.activation(out=STT[0:1, 312:320], in_=STT[0:1, 300:308],
                                         func=AF.Exp, bias=STT[0:1, 308:309],
                                         accum_out=STT[0:1, 310:311]))
        WT("vector", "sA", ex_a)
        V(lambda e: e.reciprocal(STT[0:1, 311:312], STT[0:1, 310:311]), inc=False)
        VD()
        cr_v = V(lambda e: e.tensor_scalar(out=SM[0:1, 1600:1608],
                                           in0=STT[0:1, 312:320],
                                           scalar1=STT[0:1, 311:312], scalar2=None,
                                           op0=ALU.mult))
        WT("tensor", "sV", cr_v)
        ccol_mm = MM(pf[1][0:8, 0:8], SM[0:1, 1600:1608], IDN[0:1, 0:8], start=True,
                     stop=True, inc=True)
        WT("vector", "sT", ccol_mm)
        ccol_v = V(lambda e: e.tensor_copy(SM[0:8, 1608:1609], pf[1][0:8, 0:1]))
        WT("tensor", "sV", ccol_v)
        ce_last = 0
        for c4 in range(4):
            cemm = MM(pf[2][:, 0:1], BCs[c4], SM[0:8, 1608:1609], start=True,
                      stop=True, inc=True)
            WT("vector", "sT", cemm)
            ce_last = V(lambda e, c_=c4: e.tensor_copy(CE[:, c_:c_ + 1], pf[2][:, 0:1]))
        ce_v = ce_last

        _cut(2)
        amask = A[:, 0:SPANTOT]
        khb = [A[:, 20480:24576], A[:, 24576:28672]]
        vhb = [A[:, 28672:32768], A[:, 32768:36864]]
        qbuf = [QB[:, 0:128], QB[:, 128:256]]
        psb = A[:, 38912:43008]
        ptsb = PTS[:]
        aost = [A[:, 45056:45184], A[:, 45184:45312]]
        otmp = [A[:, 45312 + 128 * k:45440 + 128 * k] for k in range(4)]
        oaccs = [Cf[:, 2048:2304], Cf[:, 2304:2560]]
        iosc = Cf[:, 0:2048]

        WT("gpsimd", "sV", cnt["sV"])
        for i in range(16):
            spanw = (BOUND[i] - CLEAR[i]) * 128
            gv = GP(lambda e, w=spanw, b=CLEAR[i] * 128: e.iota(
                iosc[:, 0:w], pattern=[[1, w]], base=b, channel_multiplier=0,
                allow_small_or_imprecise_dtypes=True))
            WT("vector", "sG", gv)
            mv = V(lambda e, i_=i, w=spanw: e.tensor_scalar(
                out=amask[:, SPANOFF[i_]:SPANOFF[i_] + w], in0=iosc[:, 0:w],
                scalar1=QIX[:, i_:i_ + 1], scalar2=-1e30, op0=ALU.is_gt,
                op1=ALU.mult))
            WT("gpsimd", "sV", mv)
        amask_v = cnt["sV"]

        def chunks_of(bound):
            ch, k0 = [], 0
            while k0 < bound:
                ch.append((k0, min(k0 + 8, bound)))
                k0 += 8
            return ch

        WT("sync", "sst", cnt["sst"])
        head_T0 = {}
        kvld = {}
        kvld[0] = (LD(khb[0][:], k_sp[0, :, :]),)
        for kt in range(32):
            LD(vhb[0][:, kt * 128:(kt + 1) * 128], v_sp[0, :, kt * 128:(kt + 1) * 128],
               tr=True)
        kvld[0] = (kvld[0][0], cnt["sx"])
        pt_evac_v = {0: 0, 1: 0}
        pv_mm_v = {0: 0, 1: 0}
        sbank_free_a = {0: 0, 1: 0}
        pv_free_a = {0: 0, 1: 0}
        ao_sto_v = [0, 0]
        smm_last = {0: 0, 1: 0}
        last_oc_v = [0]
        last_pv_T = [0]
        prev_head_exp_a = [0]

        for h in range(H):
            head_T0[h] = cnt["sT"]
            hb = h % 2
            WT("vector", "sA", prev_head_exp_a[0])
            sez_v = V(lambda e: e.memset(STT[:, 428:556], 0.0))
            VD()
            if h + 1 < H:
                nh = (h + 1) % 2
                if h >= 1:
                    WT("sync", "sT", head_T0[h])
                kvld[h + 1] = (LD(khb[nh][:], k_sp[h + 1, :, :]),)
                for kt in range(32):
                    LD(vhb[nh][:, kt * 128:(kt + 1) * 128],
                       v_sp[h + 1, :, kt * 128:(kt + 1) * 128], tr=True)
                kvld[h + 1] = (kvld[h + 1][0], cnt["sx"])
            WT("sync", "sT", head_T0[h] if h >= 1 else 0)
            qld = {0: LD(qbuf[0][:], q_sp[h, :, 0:128])}
            if "p2v" in DEBUG and h == 0:
                WT("sync", "sx", kvld[0][1])
                STO(p2v[0, :, :], vhb[0][:])
                STO(p2v[1, :, :], khb[0][:])
            for i in range(16):
                par = i % 2
                if i + 1 < 16:
                    WT("sync", "sT", smm_last[(i + 1) % 2])
                    qld[i + 1] = LD(qbuf[(i + 1) % 2][:],
                                    q_sp[h, :, (i + 1) * 128:(i + 2) * 128])
                chs = chunks_of(BOUND[i])
                nch = len(chs)
                sb1 = pf[0] if par == 0 else pf[2]
                sb2 = pf[1] if par == 0 else pf[3]
                pvb = pf[4] if par == 0 else pf[5]
                WT("tensor", "sx", max(qld[i], kvld[h][1]))
                WT("tensor", "sV", amask_v)
                WT("tensor", "sV", ce_v)
                ncol = 300 + i * 4
                scol = 364 + i * 4
                ecol = 428 + i * 4
                hcol = 492 + i * 4
                for ci, (c0, c1) in enumerate(chs):
                    if SER:
                        WT("tensor", "sV", cnt["sV"])
                        WT("tensor", "sA", cnt["sA"])
                        WT("tensor", "sx", cnt["sx"])
                        nmv2 = cnt["sT"] + 1
                        cnt["sT"] += 1
                        em("tensor", lambda e: e.nop().then_inc(sems["sT"], 1))
                        WT("vector", "sT", nmv2)
                        WT("scalar", "sT", nmv2)
                        WT("gpsimd", "sT", nmv2)
                    nkt = c1 - c0
                    WT("tensor", "sA", sbank_free_a[par])
                    for kb in range((nkt + 3) // 4):
                        w = min(4, nkt - kb * 4) * 128
                        dstp = sb1 if kb == 0 else sb2
                        smm_last[par] = MM(
                            dstp[:, 0:w], qbuf[par][:],
                            khb[hb][:, (c0 + kb * 4) * 128:(c0 + kb * 4) * 128 + w],
                            start=True, stop=True, inc=(kb == (nkt + 3) // 4 - 1))
                    smm_v = smm_last[par]
                    if "p2s" in DEBUG and h == 0 and i == 0 and ci == 0:
                        WT("vector", "sT", smm_v)
                        d1 = V(lambda e: e.tensor_copy(
                            SM[:, 0:1024].bitcast(F32), sb1[:]))
                        d2 = V(lambda e: e.tensor_copy(
                            SM[:, 1024:2048].bitcast(F32), sb2[:]))
                        WT("sync", "sV", d2)
                        stv1 = STO(p2v[0, :, 0:2048], SM[:, 0:2048])
                        STO(p2v[1, :, 0:256], QB[:])
                        stv1 = cnt["sst"]
                        WT("vector", "sst", stv1)
                        WT("scalar", "sst", stv1)
                        WT("tensor", "sst", stv1)
                    mk0 = max(CLEAR[i], c0)
                    WT("vector", "sT", smm_v)
                    if mk0 < c1:
                        def mka(e, i_=i, c0_=c0, c1_=c1, mk0_=mk0, a_=sb1, b_=sb2):
                            srcoff = SPANOFF[i_] + (mk0_ - CLEAR[i_]) * 128
                            w_ = (c1_ - mk0_) * 128
                            off = (mk0_ - c0_) * 128
                            r = None
                            if off < 512:
                                w1 = min(w_, 512 - off)
                                r = e.tensor_add(a_[:, off:off + w1],
                                                 a_[:, off:off + w1],
                                                 amask[:, srcoff:srcoff + w1])
                                srcoff += w1
                                off += w1
                                w_ -= w1
                            if w_ > 0:
                                r = e.tensor_add(b_[:, off - 512:off - 512 + w_],
                                                 b_[:, off - 512:off - 512 + w_],
                                                 amask[:, srcoff:srcoff + w_])
                            return r
                        V(mka)
                        VD()
                    def cmax(e, n=nkt, a_=sb1, b_=sb2, cc=ncol + ci):
                        if n <= 4:
                            return e.reduce_max(STT[:, cc:cc + 1], a_[:, 0:n * 128],
                                                axis=AX.X, negate=True)
                        e.reduce_max(STT[:, cc:cc + 1], a_[:, 0:512], axis=AX.X,
                                     negate=True)
                        e.reduce_max(STT[:, 296:297], b_[:, 0:(n - 4) * 128],
                                     axis=AX.X, negate=True)
                        e.drain()
                        return e.tensor_tensor(STT[:, cc:cc + 1], STT[:, cc:cc + 1],
                                               STT[:, 296:297], op=ALU.min)
                    V(cmax)
                    VD()
                    nmax_v = V(lambda e: e.engine_nop())
                    WT("scalar", "sV", nmax_v)
                    if ci == 0:
                        WT("scalar", "sV", sez_v)
                        WT("scalar", "sT", last_pv_T[0])
                    def expf(e, n=nkt, c0_=c0, a_=sb1, b_=sb2, cc=ncol + ci,
                             se=ecol + ci, sh=hcol + ci):
                        if n <= 4:
                            return e.activation(out=psb[:, c0_ * 128:(c0_ + n) * 128],
                                                in_=a_[:, 0:n * 128], func=AF.Exp,
                                                bias=STT[:, cc:cc + 1],
                                                accum_out=STT[:, se:se + 1])
                        e.activation(out=psb[:, c0_ * 128:(c0_ + 4) * 128],
                                     in_=a_[:, 0:512], func=AF.Exp,
                                     bias=STT[:, cc:cc + 1],
                                     accum_out=STT[:, sh:sh + 1])
                        return e.activation(out=psb[:, (c0_ + 4) * 128:(c0_ + n) * 128],
                                            in_=b_[:, 0:(n - 4) * 128], func=AF.Exp,
                                            bias=STT[:, cc:cc + 1],
                                            accum_out=STT[:, se:se + 1])
                    exp_a = SC(expf)
                    sbank_free_a[par] = exp_a
                    prev_head_exp_a[0] = exp_a
                    WT("tensor", "sA", exp_a)
                    if ci == 0:
                        WT("tensor", "sA", pv_free_a[par])
                    for kt in range(c0, c1):
                        WT("tensor", "sV", pt_evac_v[kt % 2])
                        trv = TR(pb[kt % 2][:, 0:128],
                                 psb[:, kt * 128:(kt + 1) * 128])
                        WT("vector", "sT", trv)
                        pt_evac_v[kt % 2] = V(lambda e, k=kt, c_=c0: e.tensor_copy(
                            ptsb[:, (k - c_) * 128:(k - c_ + 1) * 128],
                            pb[k % 2][:, 0:128]))
                    WT("tensor", "sV", pt_evac_v[(c1 - 1) % 2])
                    WT("tensor", "sV", pt_evac_v[c1 % 2])
                    if "p2pt" in DEBUG and h == 0 and i == 0 and ci == 0:
                        WT("sync", "sV", cnt["sV"])
                        stv = STO(p2v[1, :, 0:1024], ptsb)
                        WT("tensor", "sst", stv)
                    for kt in range(c0, c1):
                        pv_mm_v[par] = MM(pvb[:, ci * 128:(ci + 1) * 128],
                                          ptsb[:, (kt - c0) * 128:(kt - c0 + 1) * 128],
                                          vhb[hb][:, kt * 128:(kt + 1) * 128],
                                          start=(kt == c0), stop=(kt == c1 - 1),
                                          inc=(kt == c1 - 1))
                if SER:
                    WT("vector", "sT", cnt["sT"])
                    WT("scalar", "sT", cnt["sT"])
                pv_v = pv_mm_v[par]
                last_pv_T[0] = pv_v
                if "p2pv" in DEBUG and h == 0 and i == 0:
                    WT("vector", "sT", pv_v)
                    dbgev = V(lambda e: e.tensor_copy(
                        SM[:, 0:512].bitcast(F32), pvb[:, 0:256]))
                    WT("sync", "sV", dbgev)
                    STO(p2v[0, :, 0:512], SM[:, 0:512])
                def sefix(e, n=nch, bnd=BOUND[i], ec=ecol, hc=hcol):
                    r = e.engine_nop()
                    for c in range(n):
                        if min(8, bnd - c * 8) > 4:
                            r = e.tensor_add(STT[:, ec + c:ec + c + 1],
                                             STT[:, ec + c:ec + c + 1],
                                             STT[:, hc + c:hc + c + 1])
                    return r
                V(sefix)
                VD()
                def nfin(e, n=nch, base=ncol):
                    if n == 1:
                        return e.tensor_copy(STT[:, 260:261], STT[:, base:base + 1])
                    r = e.tensor_tensor(STT[:, 260:261], STT[:, base:base + 1],
                                        STT[:, base + 1:base + 2], op=ALU.min)
                    for c in range(2, n):
                        e.drain()
                        r = e.tensor_tensor(STT[:, 260:261], STT[:, 260:261],
                                            STT[:, base + c:base + c + 1], op=ALU.min)
                    return r
                V(nfin)
                VD()
                sd_v = V(lambda e, n=nch, base=ncol, sc=scol: e.tensor_scalar(
                    out=STT[:, sc:sc + n], in0=STT[:, base:base + n],
                    scalar1=STT[:, 260:261], scalar2=None, op0=ALU.subtract))
                WT("scalar", "sV", sd_v)
                sc_a = SC(lambda e, n=nch, sc=scol: e.activation(
                    out=STT[:, sc:sc + n], in_=STT[:, sc:sc + n], func=AF.Exp,
                    scale=-1.0))
                SD()
                WT("scalar", "sT", pv_v)
                WT("scalar", "sV", last_oc_v[0])
                for c in range(nch):
                    om_a = SC(lambda e, c_=c, p=pvb, sc=scol: e.mul(
                        out=otmp[c_][:], in_=p[:, c_ * 128:(c_ + 1) * 128],
                        mul=STT[:, sc + c_:sc + c_ + 1]), inc=(c == nch - 1))
                pv_free_a[par] = om_a
                if "p2o" in DEBUG and h == 0 and i == 0:
                    WT("sync", "sA", om_a)
                    STO(p2v[0, :, 0:512],
                        A[:, 45312:45824])
                WT("vector", "sA", om_a)
                osl = oaccs[par]
                def ocomb(e, n=nch, o=osl, ec=ecol, sc=scol):
                    if n == 1:
                        e.tensor_copy(o[:, 0:128], otmp[0][:])
                    elif n == 2:
                        e.tensor_add(o[:, 0:128], otmp[0][:], otmp[1][:])
                    elif n == 3:
                        e.tensor_add(o[:, 0:128], otmp[0][:], otmp[1][:])
                        e.drain()
                        e.tensor_add(o[:, 0:128], o[:, 0:128], otmp[2][:])
                    else:
                        e.tensor_add(o[:, 0:128], otmp[0][:], otmp[1][:])
                        e.tensor_add(o[:, 128:256], otmp[2][:], otmp[3][:])
                        e.drain()
                        e.tensor_add(o[:, 0:128], o[:, 0:128], o[:, 128:256])
                    e.drain()
                    e.tensor_mul(STT[:, 290:290 + n], STT[:, ec:ec + n],
                                 STT[:, sc:sc + n])
                    e.drain()
                    e.reduce_sum(STT[:, 261:262], STT[:, 290:290 + n], axis=AX.X)
                    e.drain()
                    e.reciprocal(STT[:, 262:263], STT[:, 261:262])
                    e.drain()
                    return e.tensor_scalar(out=o[:, 0:128], in0=o[:, 0:128],
                                           scalar1=STT[:, 262:263], scalar2=None,
                                           op0=ALU.mult)
                oc_v = V(ocomb)
                VD()
                last_oc_v[0] = oc_v
                WT("vector", "sst", ao_sto_v[par])
                if "p2f" in DEBUG and h == 0 and i == 0:
                    dv0 = V(lambda e: e.tensor_copy(SM[:, 0:512], Cf[:, 2048:2560]))
                    WT("sync", "sV", dv0)
                    STO(p2v[1, :, 0:512], SM[:, 0:512])
                ost_v = V(lambda e, p=par: e.tensor_copy(aost[p][:],
                                                         oaccs[p][:, 0:128]))
                if "p2f" in DEBUG and h == 0 and i == 0:
                    WT("sync", "sV", ost_v)
                    stx = STO(p2v[0, :, 0:128], aost[par][:])
                    WT("vector", "sst", stx)
                WT("tensor", "sV", ost_v)
                WT("tensor", "sV", pt_evac_v[par])
                otr = TR(pb[par][:, 0:128], aost[par][:])
                WT("vector", "sT", otr)
                aof_v = V(lambda e, p=par: e.tensor_copy(aost[p][:],
                                                         pb[p][:, 0:128]))
                pt_evac_v[par] = aof_v
                WT("sync", "sV", aof_v)
                ao_sto_v[par] = STO(ao_sp[h, :, i * 128:(i + 1) * 128], aost[par][:])
                if "p2dbg" in DEBUG and h == 0 and i in (0, 8):
                    di = 0 if i == 0 else 2
                    WT("sync", "sA", cnt["sA"])
                    WT("sync", "sV", cnt["sV"])
                    STO(p2dbg[di + 0, :, :], psb)
                    STO(p2dbg[di + 1, :, 0:1024], A[:, 44032:45056])
                    STO(p2dbgf[di // 2, :, :], STT[:])

        _cut(3)
        WT("sync", "sT", cnt["sT"])
        aor = A[:, 0:65536]
        WT("sync", "sst", cnt["sst"])
        WT("tensor", "sA", cnt["sA"])
        for h in range(32):
            LD(aor[:, h * 2048:(h + 1) * 2048], ao_sp[h, :, :])
        aor_v = cnt["sx"]
        w3 = [Wu[:, 0:4096], Wu[:, 4096:8192]]
        st3 = [SM[:, 0:512], SM[:, 512:1024]]
        w3_mm = {0: 0, 1: 0}
        st3_sto = [0, 0]
        for oc in range(32):
            wap = RawAP(wbf["wo"].tensor, oc * 128,
                        [[4096, 128], [128 * 4096, 32], [1, 128]])
            WT("sync", "sT", w3_mm[oc % 2])
            lv = LD(w3[oc % 2][:], wap)
            WT("tensor", "sx", max(lv, aor_v))
            for tq in range(4):
                WT("tensor", "sV", cnt["sV"])
                for c in range(32):
                    mmv = MM(pf[oc % 2][:], w3[oc % 2][:, c * 128:(c + 1) * 128],
                             aor[:, c * 2048 + tq * 512:c * 2048 + (tq + 1) * 512],
                             start=(c == 0), stop=(c == 31), inc=(c == 31))
                w3_mm[oc % 2] = mmv
                WT("vector", "sT", mmv)
                WT("vector", "sst", st3_sto[tq % 2])
                ev = V(lambda e, o=oc, q=tq: e.tensor_scalar(
                    out=st3[q % 2], in0=pf[o % 2][:],
                    scalar1=BIA[:, 96 + o:97 + o], scalar2=None, op0=ALU.add))
                WT("sync", "sV", ev)
                st3_sto[tq % 2] = STO(s0_sp[oc, :, tq * 512:(tq + 1) * 512],
                                      st3[tq % 2])

        _cut(4)
        WT("sync", "sT", cnt["sT"])
        WT("sync", "sst", cnt["sst"])
        xnr = A[:, 0:65536]
        for c in range(32):
            LD(xnr[:, c * 2048:(c + 1) * 2048], xnq_sp[c, :, :])
        xnr_v = cnt["sx"]
        hwb = Wu[:, 8192:16384]
        WT("tensor", "sA", cnt["sA"])
        dwld = {0: 0, 1: 0}
        for er in range(4):
            wap = RawAP(wbf["down"].tensor, er * 128,
                        [[512, 128], [128 * 512, 32], [1, 128]])
            WT("sync", "sT", dwld[er % 2])
            lv = LD(w3[er % 2][:], wap)
            WT("tensor", "sx", max(lv, xnr_v))
            WT("tensor", "sV", ce_v)
            for tq in range(4):
                WT("tensor", "sV", cnt["sV"])
                for c in range(32):
                    mmv = MM(pf[er % 2][:], w3[er % 2][:, c * 128:(c + 1) * 128],
                             xnr[:, c * 2048 + tq * 512:c * 2048 + (tq + 1) * 512],
                             start=(c == 0), stop=(c == 31), inc=(c == 31))
                dwld[er % 2] = mmv
                WT("vector", "sT", mmv)
                V(lambda e, r=er, q=tq: e.tensor_scalar(
                    out=hwb[:, r * 2048 + q * 512:r * 2048 + (q + 1) * 512],
                    in0=pf[r % 2][:], scalar1=CE[:, r:r + 1], scalar2=None,
                    op0=ALU.mult))
        hw_v = cnt["sV"]
        upb = SM[:, 1280:1792]
        sgtile = SM[:, 512:1024]
        s0t = [PTS[:, 0:512], PTS[:, 512:1024]]
        gst = SM[:, 0:512]
        tst = [SM[:, 1792:1920], SM[:, 1920:2048]]
        w3b_mm = {0: 0, 1: 0}
        sg_sto = {0: 0, 1: 0}
        upld = [0]
        for oc in range(32):
            wap = RawAP(wbf["worg"].tensor, oc * 128,
                        [[4096, 128], [128 * 4096, 32], [1, 128]])
            WT("sync", "sT", w3b_mm[oc % 2])
            lv = LD(w3[oc % 2][:], wap)
            uap = RawAP(wbf["up"].tensor, oc * 128,
                        [[4096, 128], [128 * 4096, 4], [1, 128]])
            WT("sync", "sT", upld[0])
            lv2 = LD(upb, uap)
            for tq in range(4):
                WT("tensor", "sx", max(lv, lv2, xnr_v))
                WT("tensor", "sV", hw_v)
                WT("tensor", "sV", cnt["sV"])
                for c in range(32):
                    MM(pf[oc % 2][:], w3[oc % 2][:, c * 128:(c + 1) * 128],
                       xnr[:, c * 2048 + tq * 512:c * 2048 + (tq + 1) * 512],
                       start=(c == 0), stop=False)
                for r in range(4):
                    mmv = MM(pf[oc % 2][:], upb[:, r * 128:(r + 1) * 128],
                             hwb[:, r * 2048 + tq * 512:r * 2048 + (tq + 1) * 512],
                             start=False, stop=(r == 3), inc=(r == 3))
                w3b_mm[oc % 2] = mmv
                upld[0] = mmv
                WT("scalar", "sT", mmv)
                g_a = SC(lambda e, o=oc: e.activation(out=gst, in_=pf[o % 2][:],
                                                      func=AF.Silu))
                s0ld = LD(s0t[tq % 2], s0_sp[oc, :, tq * 512:(tq + 1) * 512])
                WT("vector", "sx", s0ld)
                WT("vector", "sA", g_a)
                sg_v = V(lambda e, q=tq: e.tensor_mul(sgtile, gst, s0t[q % 2]))
                WT("tensor", "sV", sg_v)
                for sub in range(4):
                    WT("tensor", "sV", pt_evac_v[sub % 2])
                    trv = TR(pb[sub % 2][:, 0:128],
                             sgtile[:, sub * 128:(sub + 1) * 128])
                    WT("vector", "sT", trv)
                    WT("vector", "sst", sg_sto[sub % 2])
                    pt_evac_v[sub % 2] = V(lambda e, s2=sub % 2: e.tensor_copy(
                        tst[s2], pb[s2][:, 0:128]))
                    WT("sync", "sV", pt_evac_v[sub % 2])
                    sg_sto[sub % 2] = STO(
                        sg_sp[tq * 512 + sub * 128:tq * 512 + (sub + 1) * 128,
                              oc * 128:(oc + 1) * 128], tst[sub % 2])

        _cut(5)
        p3_done_st = cnt["sst"]
        WT("sync", "sst", p3_done_st)
        WT("sync", "sT", cnt["sT"])
        WT("vector", "sst", p3_done_st)
        fca_ld = LD(FCA, fftc)
        WT("tensor", "sx", fca_ld)
        Xr = A[:, 0:4096]
        Xi = A[:, 4096:8192]
        Hr = A[:, 8192:12288]
        Hi = A[:, 12288:16384]
        Pw = A[:, 16384:20480]
        corr = A[:, 20480:24576]
        x1b = A[:, 24576:28672]
        xn1b = A[:, 28672:32768]
        s1b = A[:, 32768:36864]
        sres = A[:, 36864:40960]
        sgld = A[:, 40960:45056]
        xqld = A[:, 45056:49152]
        w4 = [A[:, 49152:53248], A[:, 53248:57344],
              A[:, 57344:61440], A[:, 61440:65536]]

        V(lambda e: e.memset(STT[:, 0:64], 0.0))
        VD()
        z4_v = V(lambda e: e.engine_nop())
        WT("tensor", "sA", cnt["sA"])
        p4_sto = {"x1": 0, "xn1": 0, "s1": 0}

        def fwd_fft(src_tm, dstR, dstI, tag):
            for b in range(32):
                WT("tensor", "sV", pt_evac_v[b % 2])
                trv = TR(pb[b % 2][:, 0:128], src_tm[:, b:b + 4065:32])
                WT("vector", "sT", trv)
                pt_evac_v[b % 2] = V(lambda e, b_=b: e.tensor_copy(
                    w4[0][:, b_ * 128:(b_ + 1) * 128], pb[b_ % 2][:, 0:128]))
            xin_v = cnt["sV"]
            WT("tensor", "sV", xin_v)
            f1ev = {0: 0, 1: 0, 2: 0, 3: 0}
            for b in range(32):
                for side, mat, dstw in ((0, MFC[b], w4[1]), (1, MFS[b], w4[2])):
                    bk_i = (2 * b + side) % 4
                    bank = pf[bk_i]
                    WT("tensor", "sV", f1ev[bk_i])
                    ymm = MM(bank[:, 0:128], mat, w4[0][:, b * 128:(b + 1) * 128],
                             start=True, stop=True, inc=True)
                    WT("vector", "sT", ymm)
                    oap = RawAP(dstw.tensor, dstw.offset + b,
                                [[dstw.ap[0][0], 128], [32, 128]])
                    f1ev[bk_i] = V(
                        lambda e, bk=bank, o=oap: e.tensor_copy(o, bk[:, 0:128]))
            z_v = cnt["sV"]
            WT("tensor", "sV", z_v)
            for blk in range(32):
                for side, srcw, dstw in ((0, w4[1], w4[0]), (1, w4[2], w4[3])):
                    par = (2 * blk + side) % 2
                    WT("tensor", "sV", pt_evac_v[par])
                    trv = TR(pb[par][:, 0:128], srcw[:, blk * 128:(blk + 1) * 128])
                    WT("vector", "sT", trv)
                    pt_evac_v[par] = V(lambda e, b_=blk, d=dstw, p_=par: e.tensor_copy(
                        d[:, b_ * 128:(b_ + 1) * 128], pb[p_][:, 0:128]))
            zt_v = cnt["sV"]
            WT("tensor", "sV", zt_v)
            for grp in range(4):
                for side in range(2):
                    bank0 = pf[0 + 2 * side]
                    bank1 = pf[1 + 2 * side]
                    m1 = BDC
                    m2 = BDS if side == 0 else BDNS
                    s1_ = w4[0] if side == 0 else w4[3]
                    s2_ = w4[3] if side == 0 else w4[0]
                    WT("tensor", "sV", cnt["sV"])
                    for bi in range(8):
                        blk = grp * 8 + bi
                        bank = bank0 if bi < 4 else bank1
                        MM(bank[:, (bi % 4) * 128:(bi % 4 + 1) * 128], m1,
                           s1_[:, blk * 128:(blk + 1) * 128], start=True, stop=False)
                    for bi in range(8):
                        blk = grp * 8 + bi
                        bank = bank0 if bi < 4 else bank1
                        mmv = MM(bank[:, (bi % 4) * 128:(bi % 4 + 1) * 128], m2,
                                 s2_[:, blk * 128:(blk + 1) * 128], start=False,
                                 stop=True, inc=(bi == 7))
                    WT("vector", "sT", mmv)
                    dstX = dstR if side == 0 else dstI
                    V(lambda e, g=grp, d=dstX, b0=bank0: e.tensor_copy(
                        d[:, g * 1024:g * 1024 + 512], b0[:]))
                    V(lambda e, g=grp, d=dstX, b1=bank1: e.tensor_copy(
                        d[:, g * 1024 + 512:g * 1024 + 1024], b1[:]))
            return cnt["sV"]

        def inv_fft(srcR, srcI, dst_tm, real_in, tag):
            for grp in range(4):
                for side in range(2):
                    bank0 = pf[0 + 2 * side]
                    bank1 = pf[1 + 2 * side]
                    WT("tensor", "sV", cnt["sV"])
                    mmv = 0
                    for bi in range(8):
                        blk = grp * 8 + bi
                        bank = bank0 if bi < 4 else bank1
                        m1 = BDIC if side == 0 else BDIS
                        mmv = MM(bank[:, (bi % 4) * 128:(bi % 4 + 1) * 128], m1,
                                 srcR[:, blk * 128:(blk + 1) * 128], start=True,
                                 stop=real_in, inc=(real_in and bi == 7))
                        if not real_in:
                            m2 = BDINS if side == 0 else BDIC
                            mmv = MM(bank[:, (bi % 4) * 128:(bi % 4 + 1) * 128], m2,
                                     srcI[:, blk * 128:(blk + 1) * 128],
                                     start=False, stop=True, inc=(bi == 7))
                    WT("vector", "sT", mmv)
                    dstw = w4[1] if side == 0 else w4[2]
                    V(lambda e, g=grp, d=dstw, b0=bank0: e.tensor_copy(
                        d[:, g * 1024:g * 1024 + 512], b0[:]))
                    V(lambda e, g=grp, d=dstw, b1=bank1: e.tensor_copy(
                        d[:, g * 1024 + 512:g * 1024 + 1024], b1[:]))
            s_v = cnt["sV"]
            WT("tensor", "sV", s_v)
            for blk in range(32):
                for side, srcw, dstw in ((0, w4[1], w4[0]), (1, w4[2], w4[3])):
                    par = (2 * blk + side) % 2
                    WT("tensor", "sV", pt_evac_v[par])
                    trv = TR(pb[par][:, 0:128], srcw[:, blk * 128:(blk + 1) * 128])
                    WT("vector", "sT", trv)
                    oap = RawAP(dstw.tensor, dstw.offset + blk * 4,
                                [[dstw.ap[0][0], 128], [1, 4], [128, 32]])
                    pt_evac_v[par] = V(lambda e, o=oap, p_=par: e.tensor_copy(
                        o, pb[p_][:, 0:128]))
            st_v = cnt["sV"]
            WT("tensor", "sV", st_v)
            prev_tr = 0
            for u in range(32):
                bank = pf[u % 4]
                par = u % 2
                stg = SM[:, 0:128] if par == 0 else SM[:, 128:256]
                WT("tensor", "sV", cnt["sV"])
                MM(bank[:, 0:128], M2C[u], w4[0][:, u * 128:(u + 1) * 128],
                   start=True, stop=False)
                mmv = MM(bank[:, 0:128], M2S[u], w4[3][:, u * 128:(u + 1) * 128],
                         start=False, stop=True, inc=True)
                WT("vector", "sT", max(mmv, prev_tr))
                ev = V(lambda e, b=bank, sg_=stg: e.tensor_copy(sg_, b[:, 0:128]))
                WT("tensor", "sV", ev)
                WT("tensor", "sV", pt_evac_v[par])
                trv = TR(pb[par][:, 0:128], stg)
                prev_tr = trv
                WT("vector", "sT", trv)
                oap = RawAP(dst_tm.tensor, dst_tm.offset + u,
                            [[dst_tm.ap[0][0], 128], [32, 128]])
                pt_evac_v[par] = V(lambda e, o=oap, p_=par: e.tensor_copy(
                    o, pb[p_][:, 0:128]))
            return cnt["sV"]

        ln4_done = {}
        for tt in range(16):
            WT("gpsimd", "sV", cnt["sV"])
            sgl = LD(sgld, sg_sp[tt * 128:(tt + 1) * 128, :])
            xql = LDW(xqld, xq[tt * 128:(tt + 1) * 128, :])
            WT("vector", "sx", sgl)
            WT("vector", "sw", xql)
            WT("vector", "sst", cnt["sst"])
            x1_v = V(lambda e: e.tensor_add(x1b, xqld, sgld))
            VD()
            WT("scalar", "sV", max(x1_v, z4_v))
            SC(lambda e, c=tt: e.activation(out=sgld, in_=x1b, func=AF.Square,
                                            accum_out=STT[:, c:c + 1]))
            sum_v = V(lambda e, c=tt: e.reduce_sum(STT[:, 64 + 0:65], x1b, axis=AX.X))
            WT("scalar", "sV", sum_v)
            SC(lambda e: e.mul(out=STT[:, 128:129], in_=STT[:, 64:65],
                               mul=1.0 / 4096.0), inc=False)
            SC(lambda e, c=tt: e.mul(out=STT[:, 256:257], in_=STT[:, c:c + 1],
                                     mul=1.0 / 4096.0), inc=False)
            SD()
            SC(lambda e: e.activation(out=STT[:, 257:258], in_=STT[:, 128:129],
                                      func=AF.Square), inc=False)
            SD()
            SC(lambda e: e.mul(out=STT[:, 257:258], in_=STT[:, 257:258], mul=-1.0),
               inc=False)
            SD()
            SC(lambda e: e.add(out=STT[:, 258:259], in_=STT[:, 256:257],
                               add=STT[:, 257:258]), inc=False)
            SD()
            SC(lambda e: e.add(out=STT[:, 258:259], in_=STT[:, 258:259],
                               add=STT[:, 259:260]), inc=False)
            SD()
            SC(lambda e: e.activation(out=STT[:, 258:259], in_=STT[:, 258:259],
                                      func=AF.Ln), inc=False)
            SD()
            rstd_a = SC(lambda e: e.activation(out=STT[:, 192:193],
                                               in_=STT[:, 258:259], func=AF.Exp,
                                               scale=-0.5))
            WT("vector", "sA", rstd_a)
            xn1_v = V(lambda e: e.tensor_scalar(out=xn1b, in0=x1b,
                                                scalar1=STT[:, 128:129],
                                                scalar2=STT[:, 192:193],
                                                op0=ALU.subtract, op1=ALU.mult))
            WT("sync", "sV", xn1_v)
            p4_sto["x1"] = STO(x1_sp[tt * 128:(tt + 1) * 128, :], x1b)
            p4_sto["xn1"] = STO(xn1_sp[tt * 128:(tt + 1) * 128, :], xn1b)
            WT("tensor", "sV", xn1_v)
            fx_v = fwd_fft(xn1b, Xr, Xi, "x")
            WT("scalar", "sV", fx_v)
            sq1 = SC(lambda e: e.activation(out=sgld, in_=Xr, func=AF.Square))
            sq2 = SC(lambda e: e.activation(out=xqld, in_=Xi, func=AF.Square))
            WT("vector", "sA", sq2)
            p_v = V(lambda e: e.tensor_add(Pw, sgld, xqld))
            VD()
            WT("tensor", "sV", p_v)
            corr_v = inv_fft(Pw, None, corr, True, "p")
            if "p4c" in DEBUG and tt == 15:
                WT("sync", "sV", cnt["sV"])
                s1x = STO(p4dbg[0, :, :], corr)
                s2x = STO(p4dbg[1, :, :], Pw)
                WT("vector", "sst", s2x)
            VD()
            t8_v = V(lambda e: e.max(out=SM[:, 1024:1032], in_=corr))
            VD()
            V(lambda e: e.tensor_copy(STT[:, 300:308], SM[:, 1024:1032]))
            V(lambda e: e.tensor_copy(STT[:, 308:309], corr[:, 0:1]))
            VD()
            V(lambda e: e.tensor_copy(STT[:, 296:297], STT[:, 300:301]))
            VD()
            d8_v = V(lambda e: e.tensor_scalar(out=STT[:, 310:318],
                                               in0=STT[:, 300:308],
                                               scalar1=STT[:, 296:297], scalar2=None,
                                               op0=ALU.subtract))
            WT("scalar", "sV", d8_v)
            SC(lambda e: e.memzero(STT[:, 318:319]), inc=False)
            SD()
            e8_a = SC(lambda e: e.activation(out=STT[:, 320:328], in_=STT[:, 310:318],
                                             func=AF.Exp,
                                             accum_out=STT[:, 318:319]))
            WT("vector", "sA", e8_a)
            V(lambda e: e.reciprocal(STT[:, 319:320], STT[:, 318:319]), inc=False)
            VD()
            w8_v = V(lambda e: e.tensor_scalar(out=STT[:, 328:336],
                                               in0=STT[:, 320:328],
                                               scalar1=STT[:, 319:320], scalar2=None,
                                               op0=ALU.mult))
            VD()
            m8_v = V(lambda e: e.tensor_scalar(out=STT[:, 336:344],
                                               in0=STT[:, 300:308],
                                               scalar1=STT[:, 308:309], scalar2=None,
                                               op0=ALU.is_equal))
            VD()
            wm_v = V(lambda e: e.tensor_mul(STT[:, 344:352], STT[:, 328:336],
                                            STT[:, 336:344]))
            VD()
            w0s_v = V(lambda e: e.reduce_sum(STT[:, 352:353], STT[:, 344:352],
                                             axis=AX.X))
            VD()
            wp_v = V(lambda e: e.tensor_sub(STT[:, 354:362], STT[:, 328:336],
                                            STT[:, 344:352]))
            VD()
            hb_v = V(lambda e: e.memset(Hr[:], 0.0))
            VD()
            for k in range(8):
                V(lambda e, k_=k: e.tensor_scalar(out=Hi[:], in0=corr,
                                                  scalar1=STT[:, 300 + k_:301 + k_],
                                                  scalar2=STT[:, 354 + k_:355 + k_],
                                                  op0=ALU.is_equal, op1=ALU.mult))
                V(lambda e: e.tensor_add(Hr[:], Hr[:], Hi[:]))
            hpp_v = V(lambda e: e.tensor_copy(sres, Hr[:]))
            VD()
            WT("tensor", "sV", hpp_v)
            fh_v = fwd_fft(sres, Hr, Hi, "h")
            WT("vector", "sV", fh_v)
            V(lambda e: e.tensor_mul(sgld, Hr, Xr))
            V(lambda e: e.tensor_mul(xqld, Hi, Xi))
            V(lambda e: e.tensor_mul(Pw, Hr, Xi))
            gtmp_v = V(lambda e: e.tensor_mul(corr, Hi, Xr))
            VD()
            V(lambda e: e.tensor_add(Hr[:], sgld, xqld))
            gi_v = V(lambda e: e.tensor_sub(Hi[:], Pw, corr))
            VD()
            WT("tensor", "sV", gi_v)
            sres_v = inv_fft(Hr, Hi, sres, False, "g")
            VD()
            if "p4dbg" in DEBUG and tt == 15:
                WT("sync", "sV", cnt["sV"])
                WT("sync", "sA", cnt["sA"])
                STO(p4dbg[0, :, :], corr)
                STO(p4dbg[1, :, :], sres)
                STO(p4dbg[2, :, :], Hr[:])
                STO(p4dbg[3, :, :], xn1b)
                stq = STO(p4dbgf[:, :], STT[:])
                WT("vector", "sst", stq)
            WT("vector", "sV", sres_v)
            s1_v = V(lambda e: e.scalar_tensor_tensor(
                out=s1b, in0=xn1b, scalar=STT[:, 352:353], in1=sres,
                op0=ALU.mult, op1=ALU.add))
            WT("sync", "sV", s1_v)
            p4_sto["s1"] = STO(s1_sp[tt * 128:(tt + 1) * 128, :], s1b)

        _cut(6)
        p4a_done = cnt["sst"]
        WT("sync", "sst", p4a_done)
        xn1f = A[:, 0:65536]
        wb4 = [Wu[:, 0:4096], Wu[:, 4096:8192]]
        WT("gpsimd", "sst", cnt["sst"])
        WT("tensor", "sV", cnt["sV"])
        WT("tensor", "sA", cnt["sA"])
        WT("vector", "sst", cnt["sst"])
        WT("tensor", "sst", cnt["sst"])
        xload2 = {}
        xload2[0] = LD(Wu[:, 0:4096], xn1_sp[0:128, :])
        xload2[1] = LD(Wu[:, 4096:8192], xn1_sp[128:256, :])
        for tt2 in range(16):
            WT("tensor", "sx", xload2[tt2])
            srcw = Wu[:, (tt2 % 2) * 4096:(tt2 % 2) * 4096 + 4096]
            for c in range(32):
                WT("tensor", "sV", pt_evac_v[c % 2])
                trv = TR(pb[c % 2][:, 0:128], srcw[:, c * 128:(c + 1) * 128])
                WT("vector", "sT", trv)
                pt_evac_v[c % 2] = V(lambda e, c_=c, t_=tt2: e.tensor_copy(
                    xn1f[:, c_ * 2048 + t_ * 128:c_ * 2048 + t_ * 128 + 128],
                    pb[c_ % 2][:, 0:128]))
            if tt2 + 2 < 16:
                WT("sync", "sV", cnt["sV"])
                xload2[tt2 + 2] = LD(srcw, xn1_sp[(tt2 + 2) * 128:(tt2 + 3) * 128, :])
        xn1f_v = cnt["sV"]
        wgl = LDW(Wu[:, 8192:8448], RawAP(wgateT.tensor, 0,
                                          [[8, 128], [1024, 32], [1, 8]]))
        acl = LD(Wu[:, 8448:12544], RawAP(wbf["acat"].tensor, 0,
                                          [[128, 128], [128 * 128, 32], [1, 128]]))
        hw1 = Wu[:, 12544:14592]
        WT("tensor", "sw", wgl)
        WT("tensor", "sx", acl)
        for tt2 in range(16):
            WT("tensor", "sV", xn1f_v)
            WT("tensor", "sV", cnt["sV"])
            WT("tensor", "sA", cnt["sA"])
            for c in range(32):
                lg = MM(pf[4][0:8, 0:128], Wu[:, 8192 + c * 8:8192 + (c + 1) * 8],
                        xn1f[:, c * 2048 + tt2 * 128:c * 2048 + tt2 * 128 + 128],
                        start=(c == 0), stop=(c == 31), inc=(c == 31))
            WT("vector", "sT", lg)
            lg8v = V(lambda e: e.tensor_copy(SM[0:8, 384:512], pf[4][0:8, 0:128]))
            WT("tensor", "sV", lg8v)
            WT("tensor", "sV", pt_evac_v[0])
            cnt["sT"] += 1
            em("tensor", lambda e: e.transpose(
                pb[0][:, 0:8], SM[0:8, 384:512],
                IDN[0:8, 0:8]).then_inc(sems["sT"], 1))
            lgt = cnt["sT"]
            WT("vector", "sT", lgt)
            pt_evac_v[0] = V(lambda e: e.tensor_copy(STT[:, 400:408],
                                                     pb[0][:, 0:8]))
            VD()
            V(lambda e: e.reduce_max(STT[:, 408:409], STT[:, 400:408], axis=AX.X,
                                     negate=True))
            VD()
            lgm_v = V(lambda e: e.engine_nop())
            WT("scalar", "sV", lgm_v)
            SC(lambda e: e.memzero(STT[:, 409:410]), inc=False)
            SD()
            pe_a = SC(lambda e: e.activation(out=STT[:, 410:418], in_=STT[:, 400:408],
                                             func=AF.Exp, bias=STT[:, 408:409],
                                             accum_out=STT[:, 409:410]))
            WT("vector", "sA", pe_a)
            V(lambda e: e.reciprocal(STT[:, 418:419], STT[:, 409:410]), inc=False)
            VD()
            pr_v = V(lambda e: e.tensor_scalar(out=STT[:, 420:428], in0=STT[:, 410:418],
                                               scalar1=STT[:, 418:419], scalar2=None,
                                               op0=ALU.mult))
            VD()
            V(lambda e: e.max(out=STT[:, 428:436], in_=STT[:, 420:428]))
            VD()
            V(lambda e: e.tensor_add(STT[:, 436:437], STT[:, 428:429],
                                     STT[:, 429:430]))
            VD()
            V(lambda e: e.reciprocal(STT[:, 437:438], STT[:, 436:437]), inc=False)
            V(lambda e: e.tensor_scalar(out=STT[:, 440:448], in0=STT[:, 420:428],
                                        scalar1=STT[:, 428:429], scalar2=None,
                                        op0=ALU.is_equal))
            V(lambda e: e.tensor_scalar(out=STT[:, 448:456], in0=STT[:, 420:428],
                                        scalar1=STT[:, 429:430], scalar2=None,
                                        op0=ALU.is_equal))
            VD()
            V(lambda e: e.tensor_add(STT[:, 456:464], STT[:, 440:448],
                                     STT[:, 448:456]))
            VD()
            V(lambda e: e.tensor_mul(STT[:, 464:472], STT[:, 420:428],
                                     STT[:, 456:464]))
            VD()
            rw_v = V(lambda e: e.tensor_scalar(out=SM[:, 256:264], in0=STT[:, 464:472],
                                               scalar1=STT[:, 437:438], scalar2=None,
                                               op0=ALU.mult))
            WT("tensor", "sV", rw_v)
            rwt = TR(pb[0][:, 0:128], SM[:, 256:384])
            WT("vector", "sT", rwt)
            rwt_v = V(lambda e: e.tensor_copy(SM[0:8, 384:512], pb[0][0:8, 0:128]))
            WT("tensor", "sV", rwt_v)
            rwe = MM(pf[4][:, 0:128], EXP8, SM[0:8, 384:512], start=True, stop=True,
                     inc=True)
            for c in range(32):
                ha = MM(pf[5][:, 0:128], Wu[:, 8448 + c * 128:8448 + (c + 1) * 128],
                        xn1f[:, c * 2048 + tt2 * 128:c * 2048 + tt2 * 128 + 128],
                        start=(c == 0), stop=(c == 31), inc=(c == 31))
            WT("vector", "sT", max(rwe, ha))
            V(lambda e: e.tensor_copy(SM[:, 256:384], pf[4][:, 0:128]))
            V(lambda e: e.tensor_copy(SM[:, 512:640], pf[5][:, 0:128]))
            VD()
            V(lambda e, t_=tt2: e.tensor_mul(hw1[:, t_ * 128:(t_ + 1) * 128],
                                             SM[:, 512:640], SM[:, 256:384]))
        hw1_v = cnt["sV"]

        s1t = [SM[:, 640:768], SM[:, 1152:1280]]
        x1t = [SM[:, 768:896], SM[:, 1280:1408]]
        g1t = [SM[:, 896:1024], SM[:, 1408:1536]]
        outstage = Cf[:, 4096:4224]
        fin_v = {0: 0, 1: 0}
        WT("tensor", "sA", cnt["sA"])
        w4b_mm = {0: 0, 1: 0}
        out_sto = [0]
        for oc in range(32):
            wap = RawAP(wbf["wbase"].tensor, oc * 128,
                        [[4096, 128], [128 * 4096, 32], [1, 128]])
            WT("sync", "sT", w4b_mm[oc % 2])
            lv = LD(wb4[oc % 2][:], wap)
            bap = RawAP(wbf["bcat"].tensor, oc * 128, [[4096, 128], [1, 128]])
            bcl = LD(Wu[:, 8192 + (oc % 2) * 128:8320 + (oc % 2) * 128], bap)
            for tq in range(4):
                WT("tensor", "sx", max(lv, bcl))
                WT("tensor", "sV", hw1_v)
                for c in range(32):
                    MM(pf[oc % 2][:], wb4[oc % 2][:, c * 128:(c + 1) * 128],
                       xn1f[:, c * 2048 + tq * 512:c * 2048 + (tq + 1) * 512],
                       start=(c == 0), stop=False)
                mmv = MM(pf[oc % 2][:], Wu[:, 8192 + (oc % 2) * 128:8320 + (oc % 2) * 128],
                         hw1[:, tq * 512:(tq + 1) * 512], start=False, stop=True,
                         inc=True)
                w4b_mm[oc % 2] = mmv
                WT("scalar", "sT", mmv)
                g_a = SC(lambda e, o=oc: e.activation(out=gst, in_=pf[o % 2][:],
                                                      func=AF.Silu,
                                                      bias=BIA[:, 128 + o:129 + o],
                                                      scale=1.0))
                for sub in range(4):
                    t0 = tq * 512 + sub * 128
                    sp = sub % 2
                    WT("tensor", "sA", g_a)
                    WT("tensor", "sV", pt_evac_v[sp])
                    trv = TR(pb[sp][:, 0:128], gst[:, sub * 128:(sub + 1) * 128])
                    WT("vector", "sT", trv)
                    pt_evac_v[sp] = V(lambda e, s2=sp: e.tensor_copy(
                        g1t[s2], pb[s2][:, 0:128]))
                    WT("sync", "sV", fin_v[sp])
                    s1l = LD(s1t[sp], RawAP(s1_sp.tensor, t0 * 4096 + oc * 128,
                                            [[4096, 128], [1, 128]]))
                    x1l = LD(x1t[sp], RawAP(x1_sp.tensor, t0 * 4096 + oc * 128,
                                            [[4096, 128], [1, 128]]))
                    WT("vector", "sx", max(s1l, x1l))
                    V(lambda e, s2=sp: e.tensor_mul(SM[:, 1536:1664], s1t[s2],
                                                    g1t[s2]))
                    VD()
                    WT("vector", "sst", out_sto[0])
                    fin_v[sp] = V(lambda e, s2=sp: e.tensor_add(
                        outstage, x1t[s2], SM[:, 1536:1664]))
                    ov = fin_v[sp]
                    WT("sync", "sV", ov)
                    out_sto[0] = STO(RawAP(outT.tensor, t0 * 4096 + oc * 128,
                                           [[4096, 128], [1, 128]]),
                                     outstage)


    except _Cut:
        pass

    WT("sync", "sst", cnt["sst"])
    WT("gpsimd", "sst", cnt["sst"])

    blk = ctx.enter_context(nc.Block())
    blk.sync(lambda e: [f(e) for f in ops["sync"]])
    blk.gpsimd(lambda e: [f(e) for f in ops["gpsimd"]])
    blk.scalar(lambda e: [f(e) for f in ops["scalar"]])
    blk.vector(lambda e: [f(e) for f in ops["vector"]])
    blk.tensor(lambda e: [f(e) for f in ops["tensor"]])
    ctx.close()
    return nc


def _consts():
    bf = ml_dtypes.bfloat16
    ident = np.eye(128, dtype=np.float32).astype(bf)
    oh = np.zeros((128, 64), np.float32)
    for c in range(8):
        oh[:, c * 8 + c] = 1.0
    oh = oh.astype(bf)
    fc = np.zeros((128, 16384), np.float32)
    fs = np.zeros((128, 1408), np.float32)
    a = np.arange(128)
    k1 = np.arange(128)
    for b in range(32):
        ph = 2.0 * np.pi * np.outer(32 * a + b, k1) / 4096.0
        fc[:, 128 * b:128 * (b + 1)] = np.cos(ph)
        fc[:, 4096 + 128 * b:4096 + 128 * (b + 1)] = -np.sin(ph)
    p = np.arange(128)
    for u in range(32):
        ph = 2.0 * np.pi * np.outer(k1, (u + 32 * p)) / 4096.0
        fc[:, 8192 + 128 * u:8192 + 128 * (u + 1)] = np.cos(ph) / 4096.0
        fc[:, 12288 + 128 * u:12288 + 128 * (u + 1)] = np.sin(ph) / 4096.0
    bd_c = np.zeros((128, 128), np.float32)
    bd_s = np.zeros((128, 128), np.float32)
    for t in range(4):
        bb, kk = np.arange(32), np.arange(32)
        ph = 2.0 * np.pi * np.outer(bb, kk) / 32.0
        bd_c[32 * t:32 * t + 32, 32 * t:32 * t + 32] = np.cos(ph)
        bd_s[32 * t:32 * t + 32, 32 * t:32 * t + 32] = np.sin(ph)
    fs[:, 0:128] = bd_c
    fs[:, 128:256] = bd_s
    fs[:, 256:384] = -bd_s
    fs[:, 384:512] = bd_c
    fs[:, 512:640] = bd_s
    fs[:, 640:768] = -bd_s
    e8 = np.zeros((8, 128), np.float32)
    for e_ in range(8):
        e8[e_, 16 * e_:16 * (e_ + 1)] = 1.0
    fs[0:8, 768:896] = e8
    for c in range(4):
        bc = np.zeros((8, 128), np.float32)
        bc[2 * c, 0:64] = 1.0
        bc[2 * c + 1, 64:128] = 1.0
        fs[0:8, 896 + 128 * c:896 + 128 * (c + 1)] = bc
    return ident, oh, fc.astype(bf), fs.astype(bf)


def kernel(**inputs):
    f32 = np.float32
    cat = np.ascontiguousarray
    x = np.asarray(inputs["L0_attn_x"], f32)
    wq = np.asarray(inputs["L0_attn_Wq"], f32)
    wk = np.asarray(inputs["L0_attn_Wk"], f32)
    wv = np.asarray(inputs["L0_attn_Wv"], f32)
    wo = np.asarray(inputs["L0_attn_Wo"], f32)
    bq = np.asarray(inputs["L0_attn_bq"], f32)
    bk = np.asarray(inputs["L0_attn_bk"], f32)
    bv = np.asarray(inputs["L0_attn_bv"], f32)
    bo = np.asarray(inputs["L0_attn_bo"], f32)
    worg = np.asarray(inputs["L0_ffn_W_org"], f32)
    ldn = np.asarray(inputs["L0_ffn_lora_down"], f32)
    lup = np.asarray(inputs["L0_ffn_lora_up"], f32)
    lkeys = np.asarray(inputs["L0_ffn_lora_keys"], f32)
    wqr = np.asarray(inputs["L0_ffn_Wq"], f32)
    wbase = np.asarray(inputs["L1_ffn_W_base"], f32)
    bbase = np.asarray(inputs["L1_ffn_b_base"], f32)
    wgate = np.asarray(inputs["L1_ffn_W_gate"], f32)
    am = np.asarray(inputs["L1_ffn_A"], f32)
    bm = np.asarray(inputs["L1_ffn_B"], f32)

    ident, oh, fc, fs = _consts()
    biases = np.zeros((128, 160), f32)
    for col, vec in enumerate((bq, bk, bv, bo, bbase)):
        biases[:, col * 32:(col + 1) * 32] = vec.reshape(32, 128).T
    shared = {
        "wqT": cat(wq.T), "wkT": cat(wk.T), "wvT": cat(wv.T), "woT": cat(wo.T),
        "biases": biases,
        "worgT": cat(worg.T),
        "downT": cat(ldn.transpose(1, 0, 2).reshape(4096, 512)),
        "upT": cat(lup.reshape(512, 4096)),
        "keysTc": cat(lkeys.T).astype(f32),
        "wqrT": cat(wqr.T),
        "wbaseT": cat(wbase.T), "wgateT": cat(wgate.T),
        "acatT": cat(am.transpose(2, 0, 1).reshape(4096, 128)),
        "bcatT": cat(bm.transpose(0, 2, 1).reshape(128, 4096)),
        "identc": ident, "onehotc": oh, "fftc": fc, "fftsm": fs,
    }

    in_maps = []
    for core in range(8):
        b, j = core // 2, core % 2
        gset = G0 if j == 0 else G1
        rows = np.concatenate([np.arange(g * 128, (g + 1) * 128) for g in gset])
        qix = np.zeros((128, 16), f32)
        for i, g in enumerate(gset):
            qix[:, i] = np.arange(g * 128, (g + 1) * 128)
        m = dict(shared)
        m["xkv"] = cat(x[b])
        m["xq"] = cat(x[b][rows])
        m["qidx"] = qix
        in_maps.append(m)

    nc = _NCCACHE.get("nc")
    if nc is None:
        nc = build()
        _NCCACHE["nc"] = nc
    res = run_bass_kernel_spmd(nc, in_maps, core_ids=list(range(8)))
    outp = np.zeros((B, S, D), f32)
    for core in range(8):
        b, j = core // 2, core % 2
        gset = G0 if j == 0 else G1
        o = np.asarray(res.results[core]["out"], f32)
        for i, g in enumerate(gset):
            outp[b, g * 128:(g + 1) * 128, :] = o[i * 128:(i + 1) * 128, :]
    return outp


# version 02 (final)
# speedup vs baseline: 1.0303x; 1.0303x over previous
\
import math
import os
import numpy as np
import ml_dtypes
from contextlib import ExitStack
import concourse.bass as bass
import concourse.mybir as mybir
from concourse.bass_utils import run_bass_kernel_spmd
from concourse.ap import AP as RawAP

F32 = mybir.dt.float32
BF16 = mybir.dt.bfloat16
AF = mybir.ActivationFunctionType
ALU = mybir.AluOpType
AX = mybir.AxisListType

B, S, D, H = 4, 4096, 4096, 32
TQ = 2048
ISQ = 1.0 / math.sqrt(128.0)

G0 = [7, 6, 5, 4, 3, 2, 1, 0, 24, 25, 26, 27, 28, 29, 30, 31]
G1 = [8, 9, 10, 11, 12, 13, 14, 15, 23, 22, 21, 20, 19, 18, 17, 16]
BOUND = [max(a, b) + 1 for a, b in zip(G0, G1)]
CLEAR = [min(a, b) for a, b in zip(G0, G1)]
SPANOFF = np.cumsum([0] + [(b - c) * 128 for b, c in zip(BOUND, CLEAR)]).tolist()
SPANTOT = int(SPANOFF[-1])

DEBUG = [s for s in os.environ.get("MK_DEBUG", "").split(",") if s]
PHASE = int(os.environ.get("MK_PHASE", "99"))
SER = int(os.environ.get("MK_SER", "0"))
_NCCACHE = {}


class _Cut(Exception):
    pass


def build():
    nc = bass.Bass("TRN2", target_bir_lowering=False, detect_race_conditions=False)
    ctx = ExitStack()

    def dram(name, shape, dtype, kind="Internal"):
        if kind == "Internal" and name in DEBUG:
            kind = "ExternalOutput"
        return nc.dram_tensor(name, list(shape), dtype, kind=kind).ap()

    xkv = dram("xkv", (S, D), F32, "ExternalInput")
    xq = dram("xq", (TQ, D), F32, "ExternalInput")
    qidx = dram("qidx", (128, 16), F32, "ExternalInput")
    wqT = dram("wqT", (D, D), F32, "ExternalInput")
    wkT = dram("wkT", (D, D), F32, "ExternalInput")
    wvT = dram("wvT", (D, D), F32, "ExternalInput")
    woT = dram("woT", (D, D), F32, "ExternalInput")
    biases = dram("biases", (128, 160), F32, "ExternalInput")
    worgT = dram("worgT", (D, D), F32, "ExternalInput")
    downT = dram("downT", (D, 512), F32, "ExternalInput")
    upT = dram("upT", (512, D), F32, "ExternalInput")
    keysTc = dram("keysTc", (128, 8), F32, "ExternalInput")
    wqrT = dram("wqrT", (D, 128), F32, "ExternalInput")
    wbaseT = dram("wbaseT", (D, D), F32, "ExternalInput")
    wgateT = dram("wgateT", (D, 8), F32, "ExternalInput")
    acatT = dram("acatT", (D, 128), F32, "ExternalInput")
    bcatT = dram("bcatT", (128, D), F32, "ExternalInput")
    identc = dram("identc", (128, 128), BF16, "ExternalInput")
    onehotc = dram("onehotc", (128, 64), BF16, "ExternalInput")
    fftc = dram("fftc", (128, 16384), BF16, "ExternalInput")
    fftsm = dram("fftsm", (128, 1408), BF16, "ExternalInput")
    outT = dram("out", (TQ, D), F32, "ExternalOutput")

    wbf = {}
    for nm, shp in [("wq", (D, D)), ("wk", (D, D)), ("wv", (D, D)), ("wo", (D, D)),
                    ("worg", (D, D)), ("wbase", (D, D)), ("down", (D, 512)),
                    ("up", (512, D)), ("wqr", (D, 128)), ("acat", (D, 128)),
                    ("bcat", (128, D))]:
        wbf[nm] = dram(nm + "_bf", shp, BF16)
    p2dbg = dram("p2dbg", (4, 128, 4096), BF16)
    p4dbg = dram("p4dbg", (4, 128, 4096), BF16)
    p4dbgf = dram("p4dbgf", (128, 768), F32)
    p2v = dram("p2v", (2, 128, 4096), BF16)
    p2dbgf = dram("p2dbgf", (2, 128, 768), F32)
    k_sp = dram("k_sp", (H, 128, S), BF16)
    v_sp = dram("v_sp", (H, 128, S), BF16)
    q_sp = dram("q_sp", (H, 128, TQ), BF16)
    xnq_sp = dram("xnq_sp", (32, 128, TQ), BF16)
    ao_sp = dram("ao_sp", (H, 128, TQ), BF16)
    s0_sp = dram("s0_sp", (32, 128, TQ), BF16)
    sg_sp = dram("sg_sp", (TQ, D), BF16)
    x1_sp = dram("x1_sp", (TQ, D), BF16)
    xn1_sp = dram("xn1_sp", (TQ, D), BF16)
    s1_sp = dram("s1_sp", (TQ, D), BF16)
    mean_sp = dram("mean_sp", (8, 512), BF16)

    A = ctx.enter_context(nc.sbuf_tensor("Aa", [128, 65536], BF16))
    Wu = ctx.enter_context(nc.sbuf_tensor("Wu", [128, 16384], BF16))
    Cf = ctx.enter_context(nc.sbuf_tensor("Cf", [128, 8192], F32))
    STT = ctx.enter_context(nc.sbuf_tensor("STT", [128, 768], F32))
    SM = ctx.enter_context(nc.sbuf_tensor("SMM", [128, 2048], BF16))
    IDN = ctx.enter_context(nc.sbuf_tensor("IDN", [128, 128], BF16))
    OH = ctx.enter_context(nc.sbuf_tensor("OHH", [128, 64], BF16))
    ZOH = ctx.enter_context(nc.sbuf_tensor("ZOH", [128, 8], BF16))
    BIA = ctx.enter_context(nc.sbuf_tensor("BIA", [128, 160], F32))
    QIX = ctx.enter_context(nc.sbuf_tensor("QIX", [128, 16], F32))
    CE = ctx.enter_context(nc.sbuf_tensor("CEE", [128, 8], F32))
    OHX = ctx.enter_context(nc.sbuf_tensor("OHX", [128, 1408], BF16))
    QB = ctx.enter_context(nc.sbuf_tensor("QB", [128, 256], BF16))
    PTS = ctx.enter_context(nc.sbuf_tensor("PTS", [128, 1024], BF16))
    pf = [ctx.enter_context(nc.psum_tensor("pf%d" % i, [128, 512], F32))
          for i in range(6)]
    pb = [ctx.enter_context(nc.psum_tensor("pbb%d" % i, [128, 1024], BF16))
          for i in range(2)]

    sems = {}
    for s in ("sx", "sw", "sst", "sT", "sV", "sA", "sG"):
        sems[s] = ctx.enter_context(nc.semaphore(s))
    cnt = {s: 0 for s in sems}
    ops = {k: [] for k in ("sync", "gpsimd", "scalar", "vector", "tensor")}

    def em(eng, fn):
        ops[eng].append(fn)

    def WT(eng, s, val):
        if val > 0:
            em(eng, lambda e, h=sems[s], v=val: e.wait_ge(h, v))

    def LD(out_ap, in_ap, tr=False, slow=False):
        if slow:
            em("sync", lambda e, o=out_ap, i=in_ap:
               e.dma_start(out=o, in_=i,
                           allow_slow_non_contiguous=True).then_inc(sems["sx"], 16))
        else:
            em("sync", lambda e, o=out_ap, i=in_ap, t=tr:
               e.dma_start(out=o, in_=i, transpose=t).then_inc(sems["sx"], 16))
        cnt["sx"] += 16
        return cnt["sx"]

    def LDW(out_ap, in_ap):
        em("gpsimd", lambda e, o=out_ap, i=in_ap:
           e.dma_start(out=o, in_=i).then_inc(sems["sw"], 16))
        cnt["sw"] += 16
        return cnt["sw"]

    def STO(out_ap, in_ap):
        em("sync", lambda e, o=out_ap, i=in_ap:
           e.dma_start(out=o, in_=i).then_inc(sems["sst"], 16))
        cnt["sst"] += 16
        return cnt["sst"]

    def MM(outp, lhsT, rhs, start, stop, inc=False):
        if inc:
            cnt["sT"] += 1
            em("tensor", lambda e, o=outp, l=lhsT, r=rhs, a=start, b=stop:
               e.matmul(o, l, r, start=a, stop=b).then_inc(sems["sT"], 1))
            return cnt["sT"]
        em("tensor", lambda e, o=outp, l=lhsT, r=rhs, a=start, b=stop:
           e.matmul(o, l, r, start=a, stop=b))
        return cnt["sT"]

    def TR(outp, inp):
        cnt["sT"] += 1
        em("tensor", lambda e, o=outp, i=inp:
           e.transpose(o, i, IDN[:]).then_inc(sems["sT"], 1))
        return cnt["sT"]

    def V(fn, inc=True):
        if inc:
            cnt["sV"] += 1
            em("vector", lambda e, f=fn: f(e).then_inc(sems["sV"], 1))
            return cnt["sV"]
        em("vector", lambda e, f=fn: f(e))
        return cnt["sV"]

    def SC(fn, inc=True):
        if inc:
            cnt["sA"] += 1
            em("scalar", lambda e, f=fn: f(e).then_inc(sems["sA"], 1))
            return cnt["sA"]
        em("scalar", lambda e, f=fn: f(e))
        return cnt["sA"]

    def GP(fn):
        cnt["sG"] += 1
        em("gpsimd", lambda e, f=fn: f(e).then_inc(sems["sG"], 1))
        return cnt["sG"]

    def VD():
        em("vector", lambda e: e.drain())

    def SD():
        em("scalar", lambda e: e.drain())

    FCA = Wu[:, 0:16384]
    MFC = [FCA[:, 128 * b:128 * (b + 1)] for b in range(32)]
    MFS = [FCA[:, 4096 + 128 * b:4096 + 128 * (b + 1)] for b in range(32)]
    M2C = [FCA[:, 8192 + 128 * u:8192 + 128 * (u + 1)] for u in range(32)]
    M2S = [FCA[:, 12288 + 128 * u:12288 + 128 * (u + 1)] for u in range(32)]
    BDC = OHX[:, 0:128]
    BDS = OHX[:, 128:256]
    BDNS = OHX[:, 256:384]
    BDIC = OHX[:, 384:512]
    BDIS = OHX[:, 512:640]
    BDINS = OHX[:, 640:768]
    EXP8 = OHX[0:8, 768:896]
    BCs = [OHX[0:8, 896 + 128 * c:896 + 128 * (c + 1)] for c in range(4)]

    LD(IDN[:], identc)
    LD(OH[:], onehotc)
    LD(QIX[:], qidx)
    LD(BIA[:], biases)
    LD(OHX[:], fftsm)
    V(lambda e: e.memset(ZOH[:], 0.0))
    p0_lds = cnt["sx"]

    jobs = []
    for nm, src, ntile, wid in [("wq", wqT, 32, D), ("wk", wkT, 32, D),
                                ("wv", wvT, 32, D), ("wo", woT, 32, D),
                                ("worg", worgT, 32, D), ("wbase", wbaseT, 32, D),
                                ("down", downT, 32, 512), ("up", upT, 4, D),
                                ("wqr", wqrT, 32, 128), ("acat", acatT, 32, 128),
                                ("bcat", bcatT, 1, D)]:
        for t in range(ntile):
            jobs.append((nm, src, t, wid))
    st_after = {}
    for ji, (nm, src, t, wid) in enumerate(jobs):
        stg = A[:, (ji % 2) * 4096:(ji % 2) * 4096 + wid]
        if ji >= 2:
            WT("gpsimd", "sst", st_after[ji - 2])
        wv = LDW(stg, src[t * 128:(t + 1) * 128, :])
        WT("sync", "sw", wv)
        STO(wbf[nm][t * 128:(t + 1) * 128, :], stg)
        st_after[ji] = cnt["sst"]
    p0_done = cnt["sst"]
    WT("sync", "sx", p0_lds)

    def _cut(n):
        if PHASE <= n:
            raise _Cut()

    try:
        _cut(0)
        V(lambda e: e.memset(STT[:, 0:256], 0.0))
        V(lambda e: e.memset(STT[:, 259:260], 1e-5))
        VD()
        zs_v = V(lambda e: e.engine_nop())
        WT("scalar", "sV", zs_v)

        xnfm = A[:, 0:32768]
        xn_tm = A[:, 32768:36864]
        sqd = A[:, 36864:40960]
        wstr = [Wu[:, 0:4096], Wu[:, 4096:8192]]
        kst = [SM[:, 0:512], SM[:, 512:1024]]
        xbufs = [Cf[:, 0:4096]]

        def ln_tile(xbuf, scol, xn_out):
            SC(lambda e, x=xbuf, c=scol: e.activation(
                out=sqd, in_=x, func=AF.Square, accum_out=STT[:, c:c + 1]))
            sum_v = V(lambda e, x=xbuf, c=scol: e.reduce_sum(
                STT[:, 64 + c:65 + c], x, axis=AX.X))
            WT("scalar", "sV", sum_v)
            SC(lambda e, c=scol: e.mul(out=STT[:, 128 + c:129 + c],
                                       in_=STT[:, 64 + c:65 + c], mul=1.0 / 4096.0),
               inc=False)
            SC(lambda e, c=scol: e.mul(out=STT[:, 256:257], in_=STT[:, c:c + 1],
                                       mul=1.0 / 4096.0), inc=False)
            SD()
            SC(lambda e, c=scol: e.activation(out=STT[:, 257:258],
                                              in_=STT[:, 128 + c:129 + c],
                                              func=AF.Square), inc=False)
            SD()
            SC(lambda e: e.mul(out=STT[:, 257:258], in_=STT[:, 257:258], mul=-1.0),
               inc=False)
            SD()
            SC(lambda e: e.add(out=STT[:, 258:259], in_=STT[:, 256:257],
                               add=STT[:, 257:258]), inc=False)
            SD()
            SC(lambda e: e.add(out=STT[:, 258:259], in_=STT[:, 258:259],
                               add=STT[:, 259:260]), inc=False)
            SD()
            SC(lambda e: e.activation(out=STT[:, 258:259], in_=STT[:, 258:259],
                                      func=AF.Ln), inc=False)
            SD()
            rstd_a = SC(lambda e, c=scol: e.activation(
                out=STT[:, 192 + c:193 + c], in_=STT[:, 258:259], func=AF.Exp,
                scale=-0.5))
            WT("vector", "sA", rstd_a)
            return V(lambda e, x=xbuf, c=scol, xo=xn_out: e.tensor_scalar(
                out=xo, in0=x, scalar1=STT[:, 128 + c:129 + c],
                scalar2=STT[:, 192 + c:193 + c], op0=ALU.subtract, op1=ALU.mult))

        tiles = [("kv", t) for t in range(32)] + [("q", t) for t in range(16)]
        xload_v = {}
        kind0, t0_ = tiles[0]
        xload_v[0] = LD(xbufs[0][:], xkv[t0_ * 128:(t0_ + 1) * 128, :])

        mean_open = [False]
        tr_evac_v = {0: 0, 1: 0}
        proj_mm_v = {0: 0, 1: 0}
        kst_sto_v = [0, 0]
        last_tr = [0]

        for ti, (kind, t) in enumerate(tiles):
            xbuf = xbufs[0]
            WT("scalar", "sx", xload_v[ti])
            WT("vector", "sx", xload_v[ti])
            WT("vector", "sT", last_tr[0])
            xn_v = ln_tile(xbuf, ti, xn_tm)
            if ti + 1 < len(tiles):
                kind2, t2 = tiles[ti + 1]
                src2 = (xkv[t2 * 128:(t2 + 1) * 128, :] if kind2 == "kv"
                        else xq[t2 * 128:(t2 + 1) * 128, :])
                WT("sync", "sV", xn_v)
                xload_v[ti + 1] = LD(xbuf[:], src2)
            WT("tensor", "sV", xn_v)
            if kind == "kv":
                for c8 in range(8):
                    MM(pf[5][0:8, :], OH[:, c8 * 8:c8 * 8 + 8],
                       xn_tm[:, c8 * 512:(c8 + 1) * 512],
                       start=(not mean_open[0]), stop=False)
                    mean_open[0] = True
            gpos = ti % 8
            for c in range(32):
                WT("tensor", "sV", tr_evac_v[c % 2])
                trv = TR(pb[c % 2][:, 0:128], xn_tm[:, c * 128:(c + 1) * 128])
                WT("vector", "sT", trv)
                tr_evac_v[c % 2] = V(lambda e, c_=c, g_=gpos: e.tensor_copy(
                    xnfm[:, c_ * 1024 + g_ * 128:c_ * 1024 + g_ * 128 + 128],
                    pb[c_ % 2][:, 0:128]))
            last_tr[0] = cnt["sT"]

            if ti == 31:
                mv = MM(pf[5][0:8, :], ZOH[:], xn_tm[:, 0:512], start=False, stop=True,
                        inc=True)
                WT("vector", "sT", mv)
                mev = V(lambda e: e.tensor_scalar(out=SM[0:8, 1024:1536],
                                                  in0=pf[5][0:8, :],
                                                  scalar1=1.0 / 4096.0, scalar2=None,
                                                  op0=ALU.mult))
                WT("sync", "sV", mev)
                STO(mean_sp[0:8, :], SM[0:8, 1024:1536])

            if ti % 8 == 7:
                grp = ti // 8
                xn_ready = cnt["sV"]
                if kind == "kv":
                    projs = [("wk", k_sp, 2), ("wv", v_sp, 3)]
                    tok0 = grp * 1024
                else:
                    projs = [("wq", q_sp, 1)]
                    tok0 = (grp - 4) * 1024
                WT("sync", "sst", p0_done)
                for wnm, dst_sp, bcol in projs:
                    for oc in range(32):
                        wap = RawAP(wbf[wnm].tensor, oc * 128,
                                    [[4096, 128], [128 * 4096, 32], [1, 128]])
                        WT("sync", "sT", proj_mm_v[oc % 2])
                        lv = LD(wstr[oc % 2][:], wap)
                        WT("tensor", "sx", lv)
                        WT("tensor", "sV", xn_ready)
                        for th in range(2):
                            WT("tensor", "sV", cnt["sV"])
                            for c in range(32):
                                mmv = MM(pf[oc % 2][:],
                                         wstr[oc % 2][:, c * 128:(c + 1) * 128],
                                         xnfm[:, c * 1024 + th * 512:c * 1024 + (th + 1) * 512],
                                         start=(c == 0), stop=(c == 31), inc=(c == 31))
                            proj_mm_v[oc % 2] = mmv
                            WT("vector", "sT", mmv)
                            WT("vector", "sst", kst_sto_v[th])
                            if bcol == 1:
                                ev = V(lambda e, o=oc, h2=th: e.tensor_scalar(
                                    out=kst[h2], in0=pf[o % 2][:],
                                    scalar1=BIA[:, o:o + 1], scalar2=ISQ,
                                    op0=ALU.add, op1=ALU.mult))
                            else:
                                bof = 32 * (bcol - 1)
                                ev = V(lambda e, o=oc, h2=th, bo=bof: e.tensor_scalar(
                                    out=kst[h2], in0=pf[o % 2][:],
                                    scalar1=BIA[:, bo + o:bo + o + 1], scalar2=None,
                                    op0=ALU.add))
                            WT("sync", "sV", ev)
                            kst_sto_v[th] = STO(
                                dst_sp[oc, :, tok0 + th * 512:tok0 + (th + 1) * 512],
                                kst[th])
                    if wnm == "wq":
                        for c in range(32):
                            WT("sync", "sV", xn_ready)
                            STO(xnq_sp[c, :, tok0:tok0 + 1024],
                                xnfm[:, c * 1024:(c + 1) * 1024])
                WT("tensor", "sst", cnt["sst"])

        _cut(1)
        mcol = SM[:, 1536:1568]
        WT("sync", "sst", cnt["sst"])
        LD(mcol, RawAP(mean_sp.tensor, 0, [[1, 128], [128, 32]]), slow=True)
        qrw = Wu[:, 0:4096]
        LD(qrw, RawAP(wbf["wqr"].tensor, 0, [[128, 128], [128 * 128, 32], [1, 128]]))
        kye = QB[:, 0:8]
        WT("gpsimd", "sst", cnt["sst"])
        kv_ = LDW(kye, keysTc)
        rt_ld = cnt["sx"]
        WT("tensor", "sx", rt_ld)
        WT("tensor", "sw", kv_)
        WT("tensor", "sT", proj_mm_v[0])
        qcol_mm = 0
        for c in range(32):
            qcol_mm = MM(pf[0][:, 0:1], qrw[:, c * 128:(c + 1) * 128], mcol[:, c:c + 1],
                         start=(c == 0), stop=(c == 31), inc=(c == 31))
        WT("vector", "sT", qcol_mm)
        qcol_v = V(lambda e: e.tensor_copy(SM[:, 1568:1569], pf[0][:, 0:1]))
        WT("tensor", "sV", qcol_v)
        WT("tensor", "sT", proj_mm_v[1])
        sc_mm = MM(pf[1][0:1, 0:8], SM[:, 1568:1569], kye, start=True, stop=True,
                   inc=True)
        WT("vector", "sT", sc_mm)
        V(lambda e: e.tensor_scalar(out=STT[0:1, 300:308], in0=pf[1][0:1, 0:8],
                                    scalar1=ISQ, scalar2=None, op0=ALU.mult))
        VD()
        V(lambda e: e.reduce_max(STT[0:1, 308:309], STT[0:1, 300:308], axis=AX.X,
                                 negate=True))
        VD()
        rmx_v = V(lambda e: e.engine_nop())
        WT("scalar", "sV", rmx_v)
        SC(lambda e: e.memzero(STT[0:1, 310:311]), inc=False)
        SD()
        ex_a = SC(lambda e: e.activation(out=STT[0:1, 312:320], in_=STT[0:1, 300:308],
                                         func=AF.Exp, bias=STT[0:1, 308:309],
                                         accum_out=STT[0:1, 310:311]))
        WT("vector", "sA", ex_a)
        V(lambda e: e.reciprocal(STT[0:1, 311:312], STT[0:1, 310:311]), inc=False)
        VD()
        cr_v = V(lambda e: e.tensor_scalar(out=SM[0:1, 1600:1608],
                                           in0=STT[0:1, 312:320],
                                           scalar1=STT[0:1, 311:312], scalar2=None,
                                           op0=ALU.mult))
        WT("tensor", "sV", cr_v)
        ccol_mm = MM(pf[1][0:8, 0:8], SM[0:1, 1600:1608], IDN[0:1, 0:8], start=True,
                     stop=True, inc=True)
        WT("vector", "sT", ccol_mm)
        ccol_v = V(lambda e: e.tensor_copy(SM[0:8, 1608:1609], pf[1][0:8, 0:1]))
        WT("tensor", "sV", ccol_v)
        ce_last = 0
        for c4 in range(4):
            cemm = MM(pf[2][:, 0:1], BCs[c4], SM[0:8, 1608:1609], start=True,
                      stop=True, inc=True)
            WT("vector", "sT", cemm)
            ce_last = V(lambda e, c_=c4: e.tensor_copy(CE[:, c_:c_ + 1], pf[2][:, 0:1]))
        ce_v = ce_last

        _cut(2)
        amask = A[:, 0:SPANTOT]
        khb = [A[:, 20480:24576], A[:, 24576:28672]]
        vhb = [A[:, 28672:32768], A[:, 32768:36864]]
        qbuf = [QB[:, 0:128], QB[:, 128:256]]
        psb = A[:, 38912:43008]
        ptsb = PTS[:]
        aost = [A[:, 45056:45184], A[:, 45184:45312]]
        otmp = [A[:, 45312 + 128 * k:45440 + 128 * k] for k in range(4)]
        oaccs = [Cf[:, 2048:2304], Cf[:, 2304:2560]]
        iosc = Cf[:, 0:2048]

        WT("gpsimd", "sV", cnt["sV"])
        for i in range(16):
            spanw = (BOUND[i] - CLEAR[i]) * 128
            gv = GP(lambda e, w=spanw, b=CLEAR[i] * 128: e.iota(
                iosc[:, 0:w], pattern=[[1, w]], base=b, channel_multiplier=0,
                allow_small_or_imprecise_dtypes=True))
            WT("vector", "sG", gv)
            mv = V(lambda e, i_=i, w=spanw: e.tensor_scalar(
                out=amask[:, SPANOFF[i_]:SPANOFF[i_] + w], in0=iosc[:, 0:w],
                scalar1=QIX[:, i_:i_ + 1], scalar2=-1e30, op0=ALU.is_gt,
                op1=ALU.mult))
            WT("gpsimd", "sV", mv)
        amask_v = cnt["sV"]

        def chunks_of(bound):
            ch, k0 = [], 0
            while k0 < bound:
                ch.append((k0, min(k0 + 8, bound)))
                k0 += 8
            return ch

        WT("sync", "sst", cnt["sst"])
        head_T0 = {}
        kvld = {}
        kvld[0] = (LD(khb[0][:], k_sp[0, :, :]),)
        for kt in range(32):
            LD(vhb[0][:, kt * 128:(kt + 1) * 128], v_sp[0, :, kt * 128:(kt + 1) * 128],
               tr=True)
        kvld[0] = (kvld[0][0], cnt["sx"])
        pt_evac_v = {0: 0, 1: 0}
        pt_bank = [0]
        pv_mm_v = {0: 0, 1: 0}
        sbank_free_a = {0: 0, 1: 0}
        pv_free_a = {0: 0, 1: 0}
        ao_sto_v = [0, 0]
        smm_last = {0: 0, 1: 0}
        last_oc_v = [0]
        last_pv_T = [0]
        prev_head_exp_a = [0]

        for h in range(H):
            head_T0[h] = cnt["sT"]
            hb = h % 2
            WT("vector", "sA", prev_head_exp_a[0])
            sez_v = V(lambda e: e.memset(STT[:, 428:556], 0.0))
            VD()
            if h + 1 < H:
                nh = (h + 1) % 2
                if h >= 1:
                    WT("sync", "sT", head_T0[h])
                kvld[h + 1] = (LD(khb[nh][:], k_sp[h + 1, :, :]),)
                for kt in range(32):
                    LD(vhb[nh][:, kt * 128:(kt + 1) * 128],
                       v_sp[h + 1, :, kt * 128:(kt + 1) * 128], tr=True)
                kvld[h + 1] = (kvld[h + 1][0], cnt["sx"])
            WT("sync", "sT", head_T0[h] if h >= 1 else 0)
            qld = {0: LD(qbuf[0][:], q_sp[h, :, 0:128])}
            if "p2v" in DEBUG and h == 0:
                WT("sync", "sx", kvld[0][1])
                STO(p2v[0, :, :], vhb[0][:])
                STO(p2v[1, :, :], khb[0][:])
            for i in range(16):
                par = i % 2
                if i + 1 < 16:
                    WT("sync", "sT", smm_last[(i + 1) % 2])
                    qld[i + 1] = LD(qbuf[(i + 1) % 2][:],
                                    q_sp[h, :, (i + 1) * 128:(i + 2) * 128])
                chs = chunks_of(BOUND[i])
                nch = len(chs)
                sb1 = pf[0] if par == 0 else pf[2]
                sb2 = pf[1] if par == 0 else pf[3]
                pvb = pf[4] if par == 0 else pf[5]
                WT("tensor", "sx", max(qld[i], kvld[h][1]))
                WT("tensor", "sV", amask_v)
                WT("tensor", "sV", ce_v)
                ncol = 300 + i * 4
                scol = 364 + i * 4
                ecol = 428 + i * 4
                hcol = 492 + i * 4
                for ci, (c0, c1) in enumerate(chs):
                    if SER:
                        WT("tensor", "sV", cnt["sV"])
                        WT("tensor", "sA", cnt["sA"])
                        WT("tensor", "sx", cnt["sx"])
                        nmv2 = cnt["sT"] + 1
                        cnt["sT"] += 1
                        em("tensor", lambda e: e.nop().then_inc(sems["sT"], 1))
                        WT("vector", "sT", nmv2)
                        WT("scalar", "sT", nmv2)
                        WT("gpsimd", "sT", nmv2)
                    nkt = c1 - c0
                    WT("tensor", "sA", sbank_free_a[par])
                    for kb in range((nkt + 3) // 4):
                        w = min(4, nkt - kb * 4) * 128
                        dstp = sb1 if kb == 0 else sb2
                        smm_last[par] = MM(
                            dstp[:, 0:w], qbuf[par][:],
                            khb[hb][:, (c0 + kb * 4) * 128:(c0 + kb * 4) * 128 + w],
                            start=True, stop=True, inc=(kb == (nkt + 3) // 4 - 1))
                    smm_v = smm_last[par]
                    if "p2s" in DEBUG and h == 0 and i == 0 and ci == 0:
                        WT("vector", "sT", smm_v)
                        d1 = V(lambda e: e.tensor_copy(
                            SM[:, 0:1024].bitcast(F32), sb1[:]))
                        d2 = V(lambda e: e.tensor_copy(
                            SM[:, 1024:2048].bitcast(F32), sb2[:]))
                        WT("sync", "sV", d2)
                        stv1 = STO(p2v[0, :, 0:2048], SM[:, 0:2048])
                        STO(p2v[1, :, 0:256], QB[:])
                        stv1 = cnt["sst"]
                        WT("vector", "sst", stv1)
                        WT("scalar", "sst", stv1)
                        WT("tensor", "sst", stv1)
                    mk0 = max(CLEAR[i], c0)
                    WT("vector", "sT", smm_v)
                    if mk0 < c1:
                        def mka(e, i_=i, c0_=c0, c1_=c1, mk0_=mk0, a_=sb1, b_=sb2):
                            srcoff = SPANOFF[i_] + (mk0_ - CLEAR[i_]) * 128
                            w_ = (c1_ - mk0_) * 128
                            off = (mk0_ - c0_) * 128
                            r = None
                            if off < 512:
                                w1 = min(w_, 512 - off)
                                r = e.tensor_add(a_[:, off:off + w1],
                                                 a_[:, off:off + w1],
                                                 amask[:, srcoff:srcoff + w1])
                                srcoff += w1
                                off += w1
                                w_ -= w1
                            if w_ > 0:
                                r = e.tensor_add(b_[:, off - 512:off - 512 + w_],
                                                 b_[:, off - 512:off - 512 + w_],
                                                 amask[:, srcoff:srcoff + w_])
                            return r
                        V(mka)
                        VD()
                    def cmax(e, n=nkt, a_=sb1, b_=sb2, cc=ncol + ci):
                        if n <= 4:
                            return e.reduce_max(STT[:, cc:cc + 1], a_[:, 0:n * 128],
                                                axis=AX.X, negate=True)
                        e.reduce_max(STT[:, cc:cc + 1], a_[:, 0:512], axis=AX.X,
                                     negate=True)
                        e.reduce_max(STT[:, 296:297], b_[:, 0:(n - 4) * 128],
                                     axis=AX.X, negate=True)
                        e.drain()
                        return e.tensor_tensor(STT[:, cc:cc + 1], STT[:, cc:cc + 1],
                                               STT[:, 296:297], op=ALU.min)
                    V(cmax)
                    VD()
                    nmax_v = V(lambda e: e.engine_nop())
                    WT("scalar", "sV", nmax_v)
                    if ci == 0:
                        WT("scalar", "sV", sez_v)
                        WT("scalar", "sT", last_pv_T[0])
                    def expf(e, n=nkt, c0_=c0, a_=sb1, b_=sb2, cc=ncol + ci,
                             se=ecol + ci, sh=hcol + ci):
                        if n <= 4:
                            return e.activation(out=psb[:, c0_ * 128:(c0_ + n) * 128],
                                                in_=a_[:, 0:n * 128], func=AF.Exp,
                                                bias=STT[:, cc:cc + 1],
                                                accum_out=STT[:, se:se + 1])
                        e.activation(out=psb[:, c0_ * 128:(c0_ + 4) * 128],
                                     in_=a_[:, 0:512], func=AF.Exp,
                                     bias=STT[:, cc:cc + 1],
                                     accum_out=STT[:, sh:sh + 1])
                        return e.activation(out=psb[:, (c0_ + 4) * 128:(c0_ + n) * 128],
                                            in_=b_[:, 0:(n - 4) * 128], func=AF.Exp,
                                            bias=STT[:, cc:cc + 1],
                                            accum_out=STT[:, se:se + 1])
                    exp_a = SC(expf)
                    sbank_free_a[par] = exp_a
                    prev_head_exp_a[0] = exp_a
                    WT("tensor", "sA", exp_a)
                    if ci == 0:
                        WT("tensor", "sA", pv_free_a[par])
                    bank = pt_bank[0]
                    pt_bank[0] ^= 1
                    WT("tensor", "sV", pt_evac_v[bank])
                    trv = 0
                    for kt in range(c0, c1):
                        trv = TR(pb[bank][:, (kt - c0) * 128:(kt - c0 + 1) * 128],
                                 psb[:, kt * 128:(kt + 1) * 128])
                    WT("vector", "sT", trv)
                    pt_evac_v[bank] = V(lambda e, n=nkt, b=bank: e.tensor_copy(
                        ptsb[:, 0:n * 128], pb[b][:, 0:n * 128]))
                    WT("tensor", "sV", pt_evac_v[bank])
                    if "p2pt" in DEBUG and h == 0 and i == 0 and ci == 0:
                        WT("sync", "sV", cnt["sV"])
                        stv = STO(p2v[1, :, 0:1024], ptsb)
                        WT("tensor", "sst", stv)
                    for kt in range(c0, c1):
                        pv_mm_v[par] = MM(pvb[:, ci * 128:(ci + 1) * 128],
                                          ptsb[:, (kt - c0) * 128:(kt - c0 + 1) * 128],
                                          vhb[hb][:, kt * 128:(kt + 1) * 128],
                                          start=(kt == c0), stop=(kt == c1 - 1),
                                          inc=(kt == c1 - 1))
                if SER:
                    WT("vector", "sT", cnt["sT"])
                    WT("scalar", "sT", cnt["sT"])
                pv_v = pv_mm_v[par]
                last_pv_T[0] = pv_v
                if "p2pv" in DEBUG and h == 0 and i == 0:
                    WT("vector", "sT", pv_v)
                    dbgev = V(lambda e: e.tensor_copy(
                        SM[:, 0:512].bitcast(F32), pvb[:, 0:256]))
                    WT("sync", "sV", dbgev)
                    STO(p2v[0, :, 0:512], SM[:, 0:512])
                def sefix(e, n=nch, bnd=BOUND[i], ec=ecol, hc=hcol):
                    r = e.engine_nop()
                    for c in range(n):
                        if min(8, bnd - c * 8) > 4:
                            r = e.tensor_add(STT[:, ec + c:ec + c + 1],
                                             STT[:, ec + c:ec + c + 1],
                                             STT[:, hc + c:hc + c + 1])
                    return r
                V(sefix)
                VD()
                def nfin(e, n=nch, base=ncol):
                    if n == 1:
                        return e.tensor_copy(STT[:, 260:261], STT[:, base:base + 1])
                    r = e.tensor_tensor(STT[:, 260:261], STT[:, base:base + 1],
                                        STT[:, base + 1:base + 2], op=ALU.min)
                    for c in range(2, n):
                        e.drain()
                        r = e.tensor_tensor(STT[:, 260:261], STT[:, 260:261],
                                            STT[:, base + c:base + c + 1], op=ALU.min)
                    return r
                V(nfin)
                VD()
                sd_v = V(lambda e, n=nch, base=ncol, sc=scol: e.tensor_scalar(
                    out=STT[:, sc:sc + n], in0=STT[:, base:base + n],
                    scalar1=STT[:, 260:261], scalar2=None, op0=ALU.subtract))
                WT("scalar", "sV", sd_v)
                sc_a = SC(lambda e, n=nch, sc=scol: e.activation(
                    out=STT[:, sc:sc + n], in_=STT[:, sc:sc + n], func=AF.Exp,
                    scale=-1.0))
                SD()
                WT("scalar", "sT", pv_v)
                WT("scalar", "sV", last_oc_v[0])
                for c in range(nch):
                    om_a = SC(lambda e, c_=c, p=pvb, sc=scol: e.mul(
                        out=otmp[c_][:], in_=p[:, c_ * 128:(c_ + 1) * 128],
                        mul=STT[:, sc + c_:sc + c_ + 1]), inc=(c == nch - 1))
                pv_free_a[par] = om_a
                if "p2o" in DEBUG and h == 0 and i == 0:
                    WT("sync", "sA", om_a)
                    STO(p2v[0, :, 0:512],
                        A[:, 45312:45824])
                WT("vector", "sA", om_a)
                osl = oaccs[par]
                def ocomb(e, n=nch, o=osl, ec=ecol, sc=scol):
                    if n == 1:
                        e.tensor_copy(o[:, 0:128], otmp[0][:])
                    elif n == 2:
                        e.tensor_add(o[:, 0:128], otmp[0][:], otmp[1][:])
                    elif n == 3:
                        e.tensor_add(o[:, 0:128], otmp[0][:], otmp[1][:])
                        e.drain()
                        e.tensor_add(o[:, 0:128], o[:, 0:128], otmp[2][:])
                    else:
                        e.tensor_add(o[:, 0:128], otmp[0][:], otmp[1][:])
                        e.tensor_add(o[:, 128:256], otmp[2][:], otmp[3][:])
                        e.drain()
                        e.tensor_add(o[:, 0:128], o[:, 0:128], o[:, 128:256])
                    e.drain()
                    e.tensor_mul(STT[:, 290:290 + n], STT[:, ec:ec + n],
                                 STT[:, sc:sc + n])
                    e.drain()
                    e.reduce_sum(STT[:, 261:262], STT[:, 290:290 + n], axis=AX.X)
                    e.drain()
                    e.reciprocal(STT[:, 262:263], STT[:, 261:262])
                    e.drain()
                    return e.tensor_scalar(out=o[:, 0:128], in0=o[:, 0:128],
                                           scalar1=STT[:, 262:263], scalar2=None,
                                           op0=ALU.mult)
                oc_v = V(ocomb)
                VD()
                last_oc_v[0] = oc_v
                WT("vector", "sst", ao_sto_v[par])
                if "p2f" in DEBUG and h == 0 and i == 0:
                    dv0 = V(lambda e: e.tensor_copy(SM[:, 0:512], Cf[:, 2048:2560]))
                    WT("sync", "sV", dv0)
                    STO(p2v[1, :, 0:512], SM[:, 0:512])
                ost_v = V(lambda e, p=par: e.tensor_copy(aost[p][:],
                                                         oaccs[p][:, 0:128]))
                if "p2f" in DEBUG and h == 0 and i == 0:
                    WT("sync", "sV", ost_v)
                    stx = STO(p2v[0, :, 0:128], aost[par][:])
                    WT("vector", "sst", stx)
                WT("tensor", "sV", ost_v)
                bank = pt_bank[0]
                pt_bank[0] ^= 1
                WT("tensor", "sV", pt_evac_v[bank])
                otr = TR(pb[bank][:, 0:128], aost[par][:])
                WT("vector", "sT", otr)
                aof_v = V(lambda e, p=par, b=bank: e.tensor_copy(aost[p][:],
                                                                 pb[b][:, 0:128]))
                pt_evac_v[bank] = aof_v
                WT("sync", "sV", aof_v)
                ao_sto_v[par] = STO(ao_sp[h, :, i * 128:(i + 1) * 128], aost[par][:])
                if "p2dbg" in DEBUG and h == 0 and i in (0, 8):
                    di = 0 if i == 0 else 2
                    WT("sync", "sA", cnt["sA"])
                    WT("sync", "sV", cnt["sV"])
                    STO(p2dbg[di + 0, :, :], psb)
                    STO(p2dbg[di + 1, :, 0:1024], A[:, 44032:45056])
                    STO(p2dbgf[di // 2, :, :], STT[:])

        _cut(3)
        WT("sync", "sT", cnt["sT"])
        aor = A[:, 0:65536]
        WT("sync", "sst", cnt["sst"])
        WT("tensor", "sA", cnt["sA"])
        for h in range(32):
            LD(aor[:, h * 2048:(h + 1) * 2048], ao_sp[h, :, :])
        aor_v = cnt["sx"]
        w3 = [Wu[:, 0:4096], Wu[:, 4096:8192]]
        st3 = [SM[:, 0:512], SM[:, 512:1024]]
        w3_mm = {0: 0, 1: 0}
        st3_sto = [0, 0]
        for oc in range(32):
            wap = RawAP(wbf["wo"].tensor, oc * 128,
                        [[4096, 128], [128 * 4096, 32], [1, 128]])
            WT("sync", "sT", w3_mm[oc % 2])
            lv = LD(w3[oc % 2][:], wap)
            WT("tensor", "sx", max(lv, aor_v))
            for tq in range(4):
                WT("tensor", "sV", cnt["sV"])
                for c in range(32):
                    mmv = MM(pf[oc % 2][:], w3[oc % 2][:, c * 128:(c + 1) * 128],
                             aor[:, c * 2048 + tq * 512:c * 2048 + (tq + 1) * 512],
                             start=(c == 0), stop=(c == 31), inc=(c == 31))
                w3_mm[oc % 2] = mmv
                WT("vector", "sT", mmv)
                WT("vector", "sst", st3_sto[tq % 2])
                ev = V(lambda e, o=oc, q=tq: e.tensor_scalar(
                    out=st3[q % 2], in0=pf[o % 2][:],
                    scalar1=BIA[:, 96 + o:97 + o], scalar2=None, op0=ALU.add))
                WT("sync", "sV", ev)
                st3_sto[tq % 2] = STO(s0_sp[oc, :, tq * 512:(tq + 1) * 512],
                                      st3[tq % 2])

        _cut(4)
        WT("sync", "sT", cnt["sT"])
        WT("sync", "sst", cnt["sst"])
        xnr = A[:, 0:65536]
        for c in range(32):
            LD(xnr[:, c * 2048:(c + 1) * 2048], xnq_sp[c, :, :])
        xnr_v = cnt["sx"]
        hwb = Wu[:, 8192:16384]
        WT("tensor", "sA", cnt["sA"])
        dwld = {0: 0, 1: 0}
        for er in range(4):
            wap = RawAP(wbf["down"].tensor, er * 128,
                        [[512, 128], [128 * 512, 32], [1, 128]])
            WT("sync", "sT", dwld[er % 2])
            lv = LD(w3[er % 2][:], wap)
            WT("tensor", "sx", max(lv, xnr_v))
            WT("tensor", "sV", ce_v)
            for tq in range(4):
                WT("tensor", "sV", cnt["sV"])
                for c in range(32):
                    mmv = MM(pf[er % 2][:], w3[er % 2][:, c * 128:(c + 1) * 128],
                             xnr[:, c * 2048 + tq * 512:c * 2048 + (tq + 1) * 512],
                             start=(c == 0), stop=(c == 31), inc=(c == 31))
                dwld[er % 2] = mmv
                WT("vector", "sT", mmv)
                V(lambda e, r=er, q=tq: e.tensor_scalar(
                    out=hwb[:, r * 2048 + q * 512:r * 2048 + (q + 1) * 512],
                    in0=pf[r % 2][:], scalar1=CE[:, r:r + 1], scalar2=None,
                    op0=ALU.mult))
        hw_v = cnt["sV"]
        upb = SM[:, 1280:1792]
        sgtile = SM[:, 512:1024]
        s0t = [PTS[:, 0:512], PTS[:, 512:1024]]
        gst = SM[:, 0:512]
        tst = [SM[:, 1792:1920], SM[:, 1920:2048]]
        w3b_mm = {0: 0, 1: 0}
        sg_sto = {0: 0, 1: 0}
        upld = [0]
        for oc in range(32):
            wap = RawAP(wbf["worg"].tensor, oc * 128,
                        [[4096, 128], [128 * 4096, 32], [1, 128]])
            WT("sync", "sT", w3b_mm[oc % 2])
            lv = LD(w3[oc % 2][:], wap)
            uap = RawAP(wbf["up"].tensor, oc * 128,
                        [[4096, 128], [128 * 4096, 4], [1, 128]])
            WT("sync", "sT", upld[0])
            lv2 = LD(upb, uap)
            for tq in range(4):
                WT("tensor", "sx", max(lv, lv2, xnr_v))
                WT("tensor", "sV", hw_v)
                WT("tensor", "sV", cnt["sV"])
                for c in range(32):
                    MM(pf[oc % 2][:], w3[oc % 2][:, c * 128:(c + 1) * 128],
                       xnr[:, c * 2048 + tq * 512:c * 2048 + (tq + 1) * 512],
                       start=(c == 0), stop=False)
                for r in range(4):
                    mmv = MM(pf[oc % 2][:], upb[:, r * 128:(r + 1) * 128],
                             hwb[:, r * 2048 + tq * 512:r * 2048 + (tq + 1) * 512],
                             start=False, stop=(r == 3), inc=(r == 3))
                w3b_mm[oc % 2] = mmv
                upld[0] = mmv
                WT("scalar", "sT", mmv)
                g_a = SC(lambda e, o=oc: e.activation(out=gst, in_=pf[o % 2][:],
                                                      func=AF.Silu))
                s0ld = LD(s0t[tq % 2], s0_sp[oc, :, tq * 512:(tq + 1) * 512])
                WT("vector", "sx", s0ld)
                WT("vector", "sA", g_a)
                sg_v = V(lambda e, q=tq: e.tensor_mul(sgtile, gst, s0t[q % 2]))
                WT("tensor", "sV", sg_v)
                for sub in range(4):
                    WT("tensor", "sV", pt_evac_v[sub % 2])
                    trv = TR(pb[sub % 2][:, 0:128],
                             sgtile[:, sub * 128:(sub + 1) * 128])
                    WT("vector", "sT", trv)
                    WT("vector", "sst", sg_sto[sub % 2])
                    pt_evac_v[sub % 2] = V(lambda e, s2=sub % 2: e.tensor_copy(
                        tst[s2], pb[s2][:, 0:128]))
                    WT("sync", "sV", pt_evac_v[sub % 2])
                    sg_sto[sub % 2] = STO(
                        sg_sp[tq * 512 + sub * 128:tq * 512 + (sub + 1) * 128,
                              oc * 128:(oc + 1) * 128], tst[sub % 2])

        _cut(5)
        p3_done_st = cnt["sst"]
        WT("sync", "sst", p3_done_st)
        WT("sync", "sT", cnt["sT"])
        WT("vector", "sst", p3_done_st)
        fca_ld = LD(FCA, fftc)
        WT("tensor", "sx", fca_ld)
        Xr = A[:, 0:4096]
        Xi = A[:, 4096:8192]
        Hr = A[:, 8192:12288]
        Hi = A[:, 12288:16384]
        Pw = A[:, 16384:20480]
        corr = A[:, 20480:24576]
        x1b = A[:, 24576:28672]
        xn1b = A[:, 28672:32768]
        s1b = A[:, 32768:36864]
        sres = A[:, 36864:40960]
        sgld = A[:, 40960:45056]
        xqld = A[:, 45056:49152]
        w4 = [A[:, 49152:53248], A[:, 53248:57344],
              A[:, 57344:61440], A[:, 61440:65536]]

        V(lambda e: e.memset(STT[:, 0:64], 0.0))
        VD()
        z4_v = V(lambda e: e.engine_nop())
        WT("tensor", "sA", cnt["sA"])
        p4_sto = {"x1": 0, "xn1": 0, "s1": 0}

        def fwd_fft(src_tm, dstR, dstI, tag):
            for b in range(32):
                WT("tensor", "sV", pt_evac_v[b % 2])
                trv = TR(pb[b % 2][:, 0:128], src_tm[:, b:b + 4065:32])
                WT("vector", "sT", trv)
                pt_evac_v[b % 2] = V(lambda e, b_=b: e.tensor_copy(
                    w4[0][:, b_ * 128:(b_ + 1) * 128], pb[b_ % 2][:, 0:128]))
            xin_v = cnt["sV"]
            WT("tensor", "sV", xin_v)
            f1ev = {0: 0, 1: 0, 2: 0, 3: 0}
            for b in range(32):
                for side, mat, dstw in ((0, MFC[b], w4[1]), (1, MFS[b], w4[2])):
                    bk_i = (2 * b + side) % 4
                    bank = pf[bk_i]
                    WT("tensor", "sV", f1ev[bk_i])
                    ymm = MM(bank[:, 0:128], mat, w4[0][:, b * 128:(b + 1) * 128],
                             start=True, stop=True, inc=True)
                    WT("vector", "sT", ymm)
                    oap = RawAP(dstw.tensor, dstw.offset + b,
                                [[dstw.ap[0][0], 128], [32, 128]])
                    f1ev[bk_i] = V(
                        lambda e, bk=bank, o=oap: e.tensor_copy(o, bk[:, 0:128]))
            z_v = cnt["sV"]
            WT("tensor", "sV", z_v)
            for blk in range(32):
                for side, srcw, dstw in ((0, w4[1], w4[0]), (1, w4[2], w4[3])):
                    par = (2 * blk + side) % 2
                    WT("tensor", "sV", pt_evac_v[par])
                    trv = TR(pb[par][:, 0:128], srcw[:, blk * 128:(blk + 1) * 128])
                    WT("vector", "sT", trv)
                    pt_evac_v[par] = V(lambda e, b_=blk, d=dstw, p_=par: e.tensor_copy(
                        d[:, b_ * 128:(b_ + 1) * 128], pb[p_][:, 0:128]))
            zt_v = cnt["sV"]
            WT("tensor", "sV", zt_v)
            for grp in range(4):
                for side in range(2):
                    bank0 = pf[0 + 2 * side]
                    bank1 = pf[1 + 2 * side]
                    m1 = BDC
                    m2 = BDS if side == 0 else BDNS
                    s1_ = w4[0] if side == 0 else w4[3]
                    s2_ = w4[3] if side == 0 else w4[0]
                    WT("tensor", "sV", cnt["sV"])
                    for bi in range(8):
                        blk = grp * 8 + bi
                        bank = bank0 if bi < 4 else bank1
                        MM(bank[:, (bi % 4) * 128:(bi % 4 + 1) * 128], m1,
                           s1_[:, blk * 128:(blk + 1) * 128], start=True, stop=False)
                    for bi in range(8):
                        blk = grp * 8 + bi
                        bank = bank0 if bi < 4 else bank1
                        mmv = MM(bank[:, (bi % 4) * 128:(bi % 4 + 1) * 128], m2,
                                 s2_[:, blk * 128:(blk + 1) * 128], start=False,
                                 stop=True, inc=(bi == 7))
                    WT("vector", "sT", mmv)
                    dstX = dstR if side == 0 else dstI
                    V(lambda e, g=grp, d=dstX, b0=bank0: e.tensor_copy(
                        d[:, g * 1024:g * 1024 + 512], b0[:]))
                    V(lambda e, g=grp, d=dstX, b1=bank1: e.tensor_copy(
                        d[:, g * 1024 + 512:g * 1024 + 1024], b1[:]))
            return cnt["sV"]

        def inv_fft(srcR, srcI, dst_tm, real_in, tag):
            for grp in range(4):
                for side in range(2):
                    bank0 = pf[0 + 2 * side]
                    bank1 = pf[1 + 2 * side]
                    WT("tensor", "sV", cnt["sV"])
                    mmv = 0
                    for bi in range(8):
                        blk = grp * 8 + bi
                        bank = bank0 if bi < 4 else bank1
                        m1 = BDIC if side == 0 else BDIS
                        mmv = MM(bank[:, (bi % 4) * 128:(bi % 4 + 1) * 128], m1,
                                 srcR[:, blk * 128:(blk + 1) * 128], start=True,
                                 stop=real_in, inc=(real_in and bi == 7))
                        if not real_in:
                            m2 = BDINS if side == 0 else BDIC
                            mmv = MM(bank[:, (bi % 4) * 128:(bi % 4 + 1) * 128], m2,
                                     srcI[:, blk * 128:(blk + 1) * 128],
                                     start=False, stop=True, inc=(bi == 7))
                    WT("vector", "sT", mmv)
                    dstw = w4[1] if side == 0 else w4[2]
                    V(lambda e, g=grp, d=dstw, b0=bank0: e.tensor_copy(
                        d[:, g * 1024:g * 1024 + 512], b0[:]))
                    V(lambda e, g=grp, d=dstw, b1=bank1: e.tensor_copy(
                        d[:, g * 1024 + 512:g * 1024 + 1024], b1[:]))
            s_v = cnt["sV"]
            WT("tensor", "sV", s_v)
            for blk in range(32):
                for side, srcw, dstw in ((0, w4[1], w4[0]), (1, w4[2], w4[3])):
                    par = (2 * blk + side) % 2
                    WT("tensor", "sV", pt_evac_v[par])
                    trv = TR(pb[par][:, 0:128], srcw[:, blk * 128:(blk + 1) * 128])
                    WT("vector", "sT", trv)
                    oap = RawAP(dstw.tensor, dstw.offset + blk * 4,
                                [[dstw.ap[0][0], 128], [1, 4], [128, 32]])
                    pt_evac_v[par] = V(lambda e, o=oap, p_=par: e.tensor_copy(
                        o, pb[p_][:, 0:128]))
            st_v = cnt["sV"]
            WT("tensor", "sV", st_v)
            prev_tr = 0
            for u in range(32):
                bank = pf[u % 4]
                par = u % 2
                stg = SM[:, 0:128] if par == 0 else SM[:, 128:256]
                WT("tensor", "sV", cnt["sV"])
                MM(bank[:, 0:128], M2C[u], w4[0][:, u * 128:(u + 1) * 128],
                   start=True, stop=False)
                mmv = MM(bank[:, 0:128], M2S[u], w4[3][:, u * 128:(u + 1) * 128],
                         start=False, stop=True, inc=True)
                WT("vector", "sT", max(mmv, prev_tr))
                ev = V(lambda e, b=bank, sg_=stg: e.tensor_copy(sg_, b[:, 0:128]))
                WT("tensor", "sV", ev)
                WT("tensor", "sV", pt_evac_v[par])
                trv = TR(pb[par][:, 0:128], stg)
                prev_tr = trv
                WT("vector", "sT", trv)
                oap = RawAP(dst_tm.tensor, dst_tm.offset + u,
                            [[dst_tm.ap[0][0], 128], [32, 128]])
                pt_evac_v[par] = V(lambda e, o=oap, p_=par: e.tensor_copy(
                    o, pb[p_][:, 0:128]))
            return cnt["sV"]

        ln4_done = {}
        for tt in range(16):
            WT("gpsimd", "sV", cnt["sV"])
            sgl = LD(sgld, sg_sp[tt * 128:(tt + 1) * 128, :])
            xql = LDW(xqld, xq[tt * 128:(tt + 1) * 128, :])
            WT("vector", "sx", sgl)
            WT("vector", "sw", xql)
            WT("vector", "sst", cnt["sst"])
            x1_v = V(lambda e: e.tensor_add(x1b, xqld, sgld))
            VD()
            WT("scalar", "sV", max(x1_v, z4_v))
            SC(lambda e, c=tt: e.activation(out=sgld, in_=x1b, func=AF.Square,
                                            accum_out=STT[:, c:c + 1]))
            sum_v = V(lambda e, c=tt: e.reduce_sum(STT[:, 64 + 0:65], x1b, axis=AX.X))
            WT("scalar", "sV", sum_v)
            SC(lambda e: e.mul(out=STT[:, 128:129], in_=STT[:, 64:65],
                               mul=1.0 / 4096.0), inc=False)
            SC(lambda e, c=tt: e.mul(out=STT[:, 256:257], in_=STT[:, c:c + 1],
                                     mul=1.0 / 4096.0), inc=False)
            SD()
            SC(lambda e: e.activation(out=STT[:, 257:258], in_=STT[:, 128:129],
                                      func=AF.Square), inc=False)
            SD()
            SC(lambda e: e.mul(out=STT[:, 257:258], in_=STT[:, 257:258], mul=-1.0),
               inc=False)
            SD()
            SC(lambda e: e.add(out=STT[:, 258:259], in_=STT[:, 256:257],
                               add=STT[:, 257:258]), inc=False)
            SD()
            SC(lambda e: e.add(out=STT[:, 258:259], in_=STT[:, 258:259],
                               add=STT[:, 259:260]), inc=False)
            SD()
            SC(lambda e: e.activation(out=STT[:, 258:259], in_=STT[:, 258:259],
                                      func=AF.Ln), inc=False)
            SD()
            rstd_a = SC(lambda e: e.activation(out=STT[:, 192:193],
                                               in_=STT[:, 258:259], func=AF.Exp,
                                               scale=-0.5))
            WT("vector", "sA", rstd_a)
            xn1_v = V(lambda e: e.tensor_scalar(out=xn1b, in0=x1b,
                                                scalar1=STT[:, 128:129],
                                                scalar2=STT[:, 192:193],
                                                op0=ALU.subtract, op1=ALU.mult))
            WT("sync", "sV", xn1_v)
            p4_sto["x1"] = STO(x1_sp[tt * 128:(tt + 1) * 128, :], x1b)
            p4_sto["xn1"] = STO(xn1_sp[tt * 128:(tt + 1) * 128, :], xn1b)
            WT("tensor", "sV", xn1_v)
            fx_v = fwd_fft(xn1b, Xr, Xi, "x")
            WT("scalar", "sV", fx_v)
            sq1 = SC(lambda e: e.activation(out=sgld, in_=Xr, func=AF.Square))
            sq2 = SC(lambda e: e.activation(out=xqld, in_=Xi, func=AF.Square))
            WT("vector", "sA", sq2)
            p_v = V(lambda e: e.tensor_add(Pw, sgld, xqld))
            VD()
            WT("tensor", "sV", p_v)
            corr_v = inv_fft(Pw, None, corr, True, "p")
            if "p4c" in DEBUG and tt == 15:
                WT("sync", "sV", cnt["sV"])
                s1x = STO(p4dbg[0, :, :], corr)
                s2x = STO(p4dbg[1, :, :], Pw)
                WT("vector", "sst", s2x)
            VD()
            t8_v = V(lambda e: e.max(out=SM[:, 1024:1032], in_=corr))
            VD()
            V(lambda e: e.tensor_copy(STT[:, 300:308], SM[:, 1024:1032]))
            V(lambda e: e.tensor_copy(STT[:, 308:309], corr[:, 0:1]))
            VD()
            V(lambda e: e.tensor_copy(STT[:, 296:297], STT[:, 300:301]))
            VD()
            d8_v = V(lambda e: e.tensor_scalar(out=STT[:, 310:318],
                                               in0=STT[:, 300:308],
                                               scalar1=STT[:, 296:297], scalar2=None,
                                               op0=ALU.subtract))
            WT("scalar", "sV", d8_v)
            SC(lambda e: e.memzero(STT[:, 318:319]), inc=False)
            SD()
            e8_a = SC(lambda e: e.activation(out=STT[:, 320:328], in_=STT[:, 310:318],
                                             func=AF.Exp,
                                             accum_out=STT[:, 318:319]))
            WT("vector", "sA", e8_a)
            V(lambda e: e.reciprocal(STT[:, 319:320], STT[:, 318:319]), inc=False)
            VD()
            w8_v = V(lambda e: e.tensor_scalar(out=STT[:, 328:336],
                                               in0=STT[:, 320:328],
                                               scalar1=STT[:, 319:320], scalar2=None,
                                               op0=ALU.mult))
            VD()
            m8_v = V(lambda e: e.tensor_scalar(out=STT[:, 336:344],
                                               in0=STT[:, 300:308],
                                               scalar1=STT[:, 308:309], scalar2=None,
                                               op0=ALU.is_equal))
            VD()
            wm_v = V(lambda e: e.tensor_mul(STT[:, 344:352], STT[:, 328:336],
                                            STT[:, 336:344]))
            VD()
            w0s_v = V(lambda e: e.reduce_sum(STT[:, 352:353], STT[:, 344:352],
                                             axis=AX.X))
            VD()
            wp_v = V(lambda e: e.tensor_sub(STT[:, 354:362], STT[:, 328:336],
                                            STT[:, 344:352]))
            VD()
            hb_v = V(lambda e: e.memset(Hr[:], 0.0))
            VD()
            for k in range(8):
                V(lambda e, k_=k: e.tensor_scalar(out=Hi[:], in0=corr,
                                                  scalar1=STT[:, 300 + k_:301 + k_],
                                                  scalar2=STT[:, 354 + k_:355 + k_],
                                                  op0=ALU.is_equal, op1=ALU.mult))
                V(lambda e: e.tensor_add(Hr[:], Hr[:], Hi[:]))
            hpp_v = V(lambda e: e.tensor_copy(sres, Hr[:]))
            VD()
            WT("tensor", "sV", hpp_v)
            fh_v = fwd_fft(sres, Hr, Hi, "h")
            WT("vector", "sV", fh_v)
            V(lambda e: e.tensor_mul(sgld, Hr, Xr))
            V(lambda e: e.tensor_mul(xqld, Hi, Xi))
            V(lambda e: e.tensor_mul(Pw, Hr, Xi))
            gtmp_v = V(lambda e: e.tensor_mul(corr, Hi, Xr))
            VD()
            V(lambda e: e.tensor_add(Hr[:], sgld, xqld))
            gi_v = V(lambda e: e.tensor_sub(Hi[:], Pw, corr))
            VD()
            WT("tensor", "sV", gi_v)
            sres_v = inv_fft(Hr, Hi, sres, False, "g")
            VD()
            if "p4dbg" in DEBUG and tt == 15:
                WT("sync", "sV", cnt["sV"])
                WT("sync", "sA", cnt["sA"])
                STO(p4dbg[0, :, :], corr)
                STO(p4dbg[1, :, :], sres)
                STO(p4dbg[2, :, :], Hr[:])
                STO(p4dbg[3, :, :], xn1b)
                stq = STO(p4dbgf[:, :], STT[:])
                WT("vector", "sst", stq)
            WT("vector", "sV", sres_v)
            s1_v = V(lambda e: e.scalar_tensor_tensor(
                out=s1b, in0=xn1b, scalar=STT[:, 352:353], in1=sres,
                op0=ALU.mult, op1=ALU.add))
            WT("sync", "sV", s1_v)
            p4_sto["s1"] = STO(s1_sp[tt * 128:(tt + 1) * 128, :], s1b)

        _cut(6)
        p4a_done = cnt["sst"]
        WT("sync", "sst", p4a_done)
        xn1f = A[:, 0:65536]
        wb4 = [Wu[:, 0:4096], Wu[:, 4096:8192]]
        WT("gpsimd", "sst", cnt["sst"])
        WT("tensor", "sV", cnt["sV"])
        WT("tensor", "sA", cnt["sA"])
        WT("vector", "sst", cnt["sst"])
        WT("tensor", "sst", cnt["sst"])
        xload2 = {}
        xload2[0] = LD(Wu[:, 0:4096], xn1_sp[0:128, :])
        xload2[1] = LD(Wu[:, 4096:8192], xn1_sp[128:256, :])
        for tt2 in range(16):
            WT("tensor", "sx", xload2[tt2])
            srcw = Wu[:, (tt2 % 2) * 4096:(tt2 % 2) * 4096 + 4096]
            for c in range(32):
                WT("tensor", "sV", pt_evac_v[c % 2])
                trv = TR(pb[c % 2][:, 0:128], srcw[:, c * 128:(c + 1) * 128])
                WT("vector", "sT", trv)
                pt_evac_v[c % 2] = V(lambda e, c_=c, t_=tt2: e.tensor_copy(
                    xn1f[:, c_ * 2048 + t_ * 128:c_ * 2048 + t_ * 128 + 128],
                    pb[c_ % 2][:, 0:128]))
            if tt2 + 2 < 16:
                WT("sync", "sV", cnt["sV"])
                xload2[tt2 + 2] = LD(srcw, xn1_sp[(tt2 + 2) * 128:(tt2 + 3) * 128, :])
        xn1f_v = cnt["sV"]
        wgl = LDW(Wu[:, 8192:8448], RawAP(wgateT.tensor, 0,
                                          [[8, 128], [1024, 32], [1, 8]]))
        acl = LD(Wu[:, 8448:12544], RawAP(wbf["acat"].tensor, 0,
                                          [[128, 128], [128 * 128, 32], [1, 128]]))
        hw1 = Wu[:, 12544:14592]
        WT("tensor", "sw", wgl)
        WT("tensor", "sx", acl)
        for tt2 in range(16):
            WT("tensor", "sV", xn1f_v)
            WT("tensor", "sV", cnt["sV"])
            WT("tensor", "sA", cnt["sA"])
            for c in range(32):
                lg = MM(pf[4][0:8, 0:128], Wu[:, 8192 + c * 8:8192 + (c + 1) * 8],
                        xn1f[:, c * 2048 + tt2 * 128:c * 2048 + tt2 * 128 + 128],
                        start=(c == 0), stop=(c == 31), inc=(c == 31))
            WT("vector", "sT", lg)
            lg8v = V(lambda e: e.tensor_copy(SM[0:8, 384:512], pf[4][0:8, 0:128]))
            WT("tensor", "sV", lg8v)
            WT("tensor", "sV", pt_evac_v[0])
            cnt["sT"] += 1
            em("tensor", lambda e: e.transpose(
                pb[0][:, 0:8], SM[0:8, 384:512],
                IDN[0:8, 0:8]).then_inc(sems["sT"], 1))
            lgt = cnt["sT"]
            WT("vector", "sT", lgt)
            pt_evac_v[0] = V(lambda e: e.tensor_copy(STT[:, 400:408],
                                                     pb[0][:, 0:8]))
            VD()
            V(lambda e: e.reduce_max(STT[:, 408:409], STT[:, 400:408], axis=AX.X,
                                     negate=True))
            VD()
            lgm_v = V(lambda e: e.engine_nop())
            WT("scalar", "sV", lgm_v)
            SC(lambda e: e.memzero(STT[:, 409:410]), inc=False)
            SD()
            pe_a = SC(lambda e: e.activation(out=STT[:, 410:418], in_=STT[:, 400:408],
                                             func=AF.Exp, bias=STT[:, 408:409],
                                             accum_out=STT[:, 409:410]))
            WT("vector", "sA", pe_a)
            V(lambda e: e.reciprocal(STT[:, 418:419], STT[:, 409:410]), inc=False)
            VD()
            pr_v = V(lambda e: e.tensor_scalar(out=STT[:, 420:428], in0=STT[:, 410:418],
                                               scalar1=STT[:, 418:419], scalar2=None,
                                               op0=ALU.mult))
            VD()
            V(lambda e: e.max(out=STT[:, 428:436], in_=STT[:, 420:428]))
            VD()
            V(lambda e: e.tensor_add(STT[:, 436:437], STT[:, 428:429],
                                     STT[:, 429:430]))
            VD()
            V(lambda e: e.reciprocal(STT[:, 437:438], STT[:, 436:437]), inc=False)
            V(lambda e: e.tensor_scalar(out=STT[:, 440:448], in0=STT[:, 420:428],
                                        scalar1=STT[:, 428:429], scalar2=None,
                                        op0=ALU.is_equal))
            V(lambda e: e.tensor_scalar(out=STT[:, 448:456], in0=STT[:, 420:428],
                                        scalar1=STT[:, 429:430], scalar2=None,
                                        op0=ALU.is_equal))
            VD()
            V(lambda e: e.tensor_add(STT[:, 456:464], STT[:, 440:448],
                                     STT[:, 448:456]))
            VD()
            V(lambda e: e.tensor_mul(STT[:, 464:472], STT[:, 420:428],
                                     STT[:, 456:464]))
            VD()
            rw_v = V(lambda e: e.tensor_scalar(out=SM[:, 256:264], in0=STT[:, 464:472],
                                               scalar1=STT[:, 437:438], scalar2=None,
                                               op0=ALU.mult))
            WT("tensor", "sV", rw_v)
            rwt = TR(pb[0][:, 0:128], SM[:, 256:384])
            WT("vector", "sT", rwt)
            rwt_v = V(lambda e: e.tensor_copy(SM[0:8, 384:512], pb[0][0:8, 0:128]))
            WT("tensor", "sV", rwt_v)
            rwe = MM(pf[4][:, 0:128], EXP8, SM[0:8, 384:512], start=True, stop=True,
                     inc=True)
            for c in range(32):
                ha = MM(pf[5][:, 0:128], Wu[:, 8448 + c * 128:8448 + (c + 1) * 128],
                        xn1f[:, c * 2048 + tt2 * 128:c * 2048 + tt2 * 128 + 128],
                        start=(c == 0), stop=(c == 31), inc=(c == 31))
            WT("vector", "sT", max(rwe, ha))
            V(lambda e: e.tensor_copy(SM[:, 256:384], pf[4][:, 0:128]))
            V(lambda e: e.tensor_copy(SM[:, 512:640], pf[5][:, 0:128]))
            VD()
            V(lambda e, t_=tt2: e.tensor_mul(hw1[:, t_ * 128:(t_ + 1) * 128],
                                             SM[:, 512:640], SM[:, 256:384]))
        hw1_v = cnt["sV"]

        s1t = [SM[:, 640:768], SM[:, 1152:1280]]
        x1t = [SM[:, 768:896], SM[:, 1280:1408]]
        g1t = [SM[:, 896:1024], SM[:, 1408:1536]]
        outstage = Cf[:, 4096:4224]
        fin_v = {0: 0, 1: 0}
        WT("tensor", "sA", cnt["sA"])
        w4b_mm = {0: 0, 1: 0}
        out_sto = [0]
        for oc in range(32):
            wap = RawAP(wbf["wbase"].tensor, oc * 128,
                        [[4096, 128], [128 * 4096, 32], [1, 128]])
            WT("sync", "sT", w4b_mm[oc % 2])
            lv = LD(wb4[oc % 2][:], wap)
            bap = RawAP(wbf["bcat"].tensor, oc * 128, [[4096, 128], [1, 128]])
            bcl = LD(Wu[:, 8192 + (oc % 2) * 128:8320 + (oc % 2) * 128], bap)
            for tq in range(4):
                WT("tensor", "sx", max(lv, bcl))
                WT("tensor", "sV", hw1_v)
                for c in range(32):
                    MM(pf[oc % 2][:], wb4[oc % 2][:, c * 128:(c + 1) * 128],
                       xn1f[:, c * 2048 + tq * 512:c * 2048 + (tq + 1) * 512],
                       start=(c == 0), stop=False)
                mmv = MM(pf[oc % 2][:], Wu[:, 8192 + (oc % 2) * 128:8320 + (oc % 2) * 128],
                         hw1[:, tq * 512:(tq + 1) * 512], start=False, stop=True,
                         inc=True)
                w4b_mm[oc % 2] = mmv
                WT("scalar", "sT", mmv)
                g_a = SC(lambda e, o=oc: e.activation(out=gst, in_=pf[o % 2][:],
                                                      func=AF.Silu,
                                                      bias=BIA[:, 128 + o:129 + o],
                                                      scale=1.0))
                for sub in range(4):
                    t0 = tq * 512 + sub * 128
                    sp = sub % 2
                    WT("tensor", "sA", g_a)
                    WT("tensor", "sV", pt_evac_v[sp])
                    trv = TR(pb[sp][:, 0:128], gst[:, sub * 128:(sub + 1) * 128])
                    WT("vector", "sT", trv)
                    pt_evac_v[sp] = V(lambda e, s2=sp: e.tensor_copy(
                        g1t[s2], pb[s2][:, 0:128]))
                    WT("sync", "sV", fin_v[sp])
                    s1l = LD(s1t[sp], RawAP(s1_sp.tensor, t0 * 4096 + oc * 128,
                                            [[4096, 128], [1, 128]]))
                    x1l = LD(x1t[sp], RawAP(x1_sp.tensor, t0 * 4096 + oc * 128,
                                            [[4096, 128], [1, 128]]))
                    WT("vector", "sx", max(s1l, x1l))
                    V(lambda e, s2=sp: e.tensor_mul(SM[:, 1536:1664], s1t[s2],
                                                    g1t[s2]))
                    VD()
                    WT("vector", "sst", out_sto[0])
                    fin_v[sp] = V(lambda e, s2=sp: e.tensor_add(
                        outstage, x1t[s2], SM[:, 1536:1664]))
                    ov = fin_v[sp]
                    WT("sync", "sV", ov)
                    out_sto[0] = STO(RawAP(outT.tensor, t0 * 4096 + oc * 128,
                                           [[4096, 128], [1, 128]]),
                                     outstage)


    except _Cut:
        pass

    WT("sync", "sst", cnt["sst"])
    WT("gpsimd", "sst", cnt["sst"])

    blk = ctx.enter_context(nc.Block())
    blk.sync(lambda e: [f(e) for f in ops["sync"]])
    blk.gpsimd(lambda e: [f(e) for f in ops["gpsimd"]])
    blk.scalar(lambda e: [f(e) for f in ops["scalar"]])
    blk.vector(lambda e: [f(e) for f in ops["vector"]])
    blk.tensor(lambda e: [f(e) for f in ops["tensor"]])
    ctx.close()
    return nc


def _consts():
    bf = ml_dtypes.bfloat16
    ident = np.eye(128, dtype=np.float32).astype(bf)
    oh = np.zeros((128, 64), np.float32)
    for c in range(8):
        oh[:, c * 8 + c] = 1.0
    oh = oh.astype(bf)
    fc = np.zeros((128, 16384), np.float32)
    fs = np.zeros((128, 1408), np.float32)
    a = np.arange(128)
    k1 = np.arange(128)
    for b in range(32):
        ph = 2.0 * np.pi * np.outer(32 * a + b, k1) / 4096.0
        fc[:, 128 * b:128 * (b + 1)] = np.cos(ph)
        fc[:, 4096 + 128 * b:4096 + 128 * (b + 1)] = -np.sin(ph)
    p = np.arange(128)
    for u in range(32):
        ph = 2.0 * np.pi * np.outer(k1, (u + 32 * p)) / 4096.0
        fc[:, 8192 + 128 * u:8192 + 128 * (u + 1)] = np.cos(ph) / 4096.0
        fc[:, 12288 + 128 * u:12288 + 128 * (u + 1)] = np.sin(ph) / 4096.0
    bd_c = np.zeros((128, 128), np.float32)
    bd_s = np.zeros((128, 128), np.float32)
    for t in range(4):
        bb, kk = np.arange(32), np.arange(32)
        ph = 2.0 * np.pi * np.outer(bb, kk) / 32.0
        bd_c[32 * t:32 * t + 32, 32 * t:32 * t + 32] = np.cos(ph)
        bd_s[32 * t:32 * t + 32, 32 * t:32 * t + 32] = np.sin(ph)
    fs[:, 0:128] = bd_c
    fs[:, 128:256] = bd_s
    fs[:, 256:384] = -bd_s
    fs[:, 384:512] = bd_c
    fs[:, 512:640] = bd_s
    fs[:, 640:768] = -bd_s
    e8 = np.zeros((8, 128), np.float32)
    for e_ in range(8):
        e8[e_, 16 * e_:16 * (e_ + 1)] = 1.0
    fs[0:8, 768:896] = e8
    for c in range(4):
        bc = np.zeros((8, 128), np.float32)
        bc[2 * c, 0:64] = 1.0
        bc[2 * c + 1, 64:128] = 1.0
        fs[0:8, 896 + 128 * c:896 + 128 * (c + 1)] = bc
    return ident, oh, fc.astype(bf), fs.astype(bf)


def kernel(**inputs):
    f32 = np.float32
    cat = np.ascontiguousarray
    x = np.asarray(inputs["L0_attn_x"], f32)
    wq = np.asarray(inputs["L0_attn_Wq"], f32)
    wk = np.asarray(inputs["L0_attn_Wk"], f32)
    wv = np.asarray(inputs["L0_attn_Wv"], f32)
    wo = np.asarray(inputs["L0_attn_Wo"], f32)
    bq = np.asarray(inputs["L0_attn_bq"], f32)
    bk = np.asarray(inputs["L0_attn_bk"], f32)
    bv = np.asarray(inputs["L0_attn_bv"], f32)
    bo = np.asarray(inputs["L0_attn_bo"], f32)
    worg = np.asarray(inputs["L0_ffn_W_org"], f32)
    ldn = np.asarray(inputs["L0_ffn_lora_down"], f32)
    lup = np.asarray(inputs["L0_ffn_lora_up"], f32)
    lkeys = np.asarray(inputs["L0_ffn_lora_keys"], f32)
    wqr = np.asarray(inputs["L0_ffn_Wq"], f32)
    wbase = np.asarray(inputs["L1_ffn_W_base"], f32)
    bbase = np.asarray(inputs["L1_ffn_b_base"], f32)
    wgate = np.asarray(inputs["L1_ffn_W_gate"], f32)
    am = np.asarray(inputs["L1_ffn_A"], f32)
    bm = np.asarray(inputs["L1_ffn_B"], f32)

    ident, oh, fc, fs = _consts()
    biases = np.zeros((128, 160), f32)
    for col, vec in enumerate((bq, bk, bv, bo, bbase)):
        biases[:, col * 32:(col + 1) * 32] = vec.reshape(32, 128).T
    shared = {
        "wqT": cat(wq.T), "wkT": cat(wk.T), "wvT": cat(wv.T), "woT": cat(wo.T),
        "biases": biases,
        "worgT": cat(worg.T),
        "downT": cat(ldn.transpose(1, 0, 2).reshape(4096, 512)),
        "upT": cat(lup.reshape(512, 4096)),
        "keysTc": cat(lkeys.T).astype(f32),
        "wqrT": cat(wqr.T),
        "wbaseT": cat(wbase.T), "wgateT": cat(wgate.T),
        "acatT": cat(am.transpose(2, 0, 1).reshape(4096, 128)),
        "bcatT": cat(bm.transpose(0, 2, 1).reshape(128, 4096)),
        "identc": ident, "onehotc": oh, "fftc": fc, "fftsm": fs,
    }

    in_maps = []
    for core in range(8):
        b, j = core // 2, core % 2
        gset = G0 if j == 0 else G1
        rows = np.concatenate([np.arange(g * 128, (g + 1) * 128) for g in gset])
        qix = np.zeros((128, 16), f32)
        for i, g in enumerate(gset):
            qix[:, i] = np.arange(g * 128, (g + 1) * 128)
        m = dict(shared)
        m["xkv"] = cat(x[b])
        m["xq"] = cat(x[b][rows])
        m["qidx"] = qix
        in_maps.append(m)

    nc = _NCCACHE.get("nc")
    if nc is None:
        nc = build()
        _NCCACHE["nc"] = nc
    res = run_bass_kernel_spmd(nc, in_maps, core_ids=list(range(8)))
    outp = np.zeros((B, S, D), f32)
    for core in range(8):
        b, j = core // 2, core % 2
        gset = G0 if j == 0 else G1
        o = np.asarray(res.results[core]["out"], f32)
        for i, g in enumerate(gset):
            outp[b, g * 128:(g + 1) * 128, :] = o[i * 128:(i + 1) * 128, :]
    return outp
